# version 36
# baseline (speedup 1.0000x reference)
import contextlib
import math

import numpy as np
import ml_dtypes

import concourse.bass as bass
import concourse.mybir as mybir
from concourse.bass_utils import run_bass_kernel_spmd

F32 = mybir.dt.float32
BF16 = mybir.dt.bfloat16
U8 = mybir.dt.uint8
AF = mybir.ActivationFunctionType
ALU = mybir.AluOpType
AX = mybir.AxisListType
bfdt = ml_dtypes.bfloat16

E = 2048
H = 16
D = 128
C = 128
T = 2048
HID = 8192
NSEQ = 2
NCPS = T // C
NCH = NSEQ * NCPS
LN_EPS = 1e-5
RMS_EPS = 1e-5
FFN_EPS = float(np.finfo(np.float32).eps)
MAGIC = 12582912.0

CFG = dict(
    phases=("P1", "P2", "P2b", "P3", "P4"),
    debug_scratch=False,
    nchunks=NCH,
    cores=None,
)

ENGS = ("pe", "dve", "act", "pool", "sp")
NDMASEM = 4
DMA_QUEUES = ("sp", "act", "pool")
SEM_NAMES = ("pe", "dve", "act", "pool") + tuple(
    f"d_{q}{i}" for q in DMA_QUEUES for i in range(NDMASEM))
DSIZE = {F32: 4, BF16: 2, U8: 1}


class Prog:
    def __init__(self):
        self.stream = {e: [] for e in ENGS}
        self.cnt = {s: 0 for s in SEM_NAMES}
        self.hwm = {}
        self.dma_rot = {q: 0 for q in DMA_QUEUES}

    def _waits(self, eng, after):
        out = []
        for tok in after:
            if tok is None:
                continue
            sem, val = tok
            if val <= 0:
                continue
            k = (eng, sem)
            if self.hwm.get(k, 0) >= val:
                continue
            self.hwm[k] = val
            out.append((sem, val))
        return out

    def op(self, eng, fn, after=(), inc=True):
        if eng in ("dve", "act", "pool") and self.cnt[eng] > 0:
            after = list(after) + [(eng, self.cnt[eng])]
        waits = self._waits(eng, after)
        rec_inc = None
        tok = None
        if inc:
            self.cnt[eng] += 1
            tok = (eng, self.cnt[eng])
            rec_inc = (eng, 1)
        self.stream[eng].append((fn, waits, rec_inc))
        return tok

    def dma(self, queue, dst, src, after=()):
        idx = self.dma_rot[queue]
        self.dma_rot[queue] = (idx + 1) % NDMASEM
        sem = f"d_{queue}{idx}"
        after = list(after) + [(sem, self.cnt[sem])]
        waits = self._waits(queue, after)
        self.cnt[sem] += 16
        tok = (sem, self.cnt[sem])
        self.stream[queue].append(
            (lambda eng, d=dst, s=src: eng.dma_start(d, s), waits, (sem, 16))
        )
        return tok

    def fence(self, eng, after):
        waits = self._waits(eng, after)
        if waits:
            self.stream[eng].append((None, waits, None))

    def barrier(self):
        toks = [(s, self.cnt[s]) for s in SEM_NAMES]
        for e in ENGS:
            self.fence(e, toks)

    def emit(self, nc, sems):
        with nc.Block() as blk:
            decorators = {
                "pe": blk.tensor, "dve": blk.vector, "act": blk.scalar,
                "pool": blk.gpsimd, "sp": blk.sync,
            }
            for engname in ENGS:
                items = self.stream[engname]
                if not items:
                    continue

                def body(eng, items=items):
                    for fn, waits, inc in items:
                        for (sem, val) in waits:
                            eng.wait_ge(sems[sem], val)
                        if fn is None:
                            continue
                        inst = fn(eng)
                        if inc is not None:
                            sem, amount = inc
                            inst.then_inc(sems[sem], amount)

                decorators[engname](body)


class Buf:
    def __init__(self, ap):
        self.t = ap
        self.ready = None
        self.users = []

    def war(self):
        toks = list(self.users)
        if self.ready is not None:
            toks.append(self.ready)
        return toks

    def set(self, tok):
        self.ready = tok
        self.users = []

    def use(self, tok):
        if tok is not None:
            self.users.append(tok)


class View:

    def __init__(self, buf, ap):
        self.b = buf
        self.t = ap

    @property
    def ready(self):
        return self.b.ready

    def war(self):
        return self.b.war()

    def set(self, tok):
        self.b.set(tok)

    def use(self, tok):
        self.b.use(tok)


class Arena:

    def __init__(self, tile, base=0):
        self.tile = tile
        self.limit = tile.shape[1]
        self.base = base
        self.off = base

    def reset(self):
        self.off = self.base

    def carve(self, nbytes):
        off = (self.off + 63) & ~63
        assert off + nbytes <= self.limit, f"arena overflow: {off}+{nbytes} > {self.limit}"
        self.off = off + nbytes
        return off

    def ap(self, shape, dtype):
        n = shape[1] * DSIZE[dtype]
        off = self.carve(n)
        v = self.tile[0:shape[0], off:off + n]
        if dtype != U8:
            v = v.bitcast(dtype)
        return v

    def buf(self, shape, dtype):
        return Buf(self.ap(shape, dtype))


def _host_constants():
    j = np.arange(D // 4, dtype=np.float64)
    freq = (1.0 / 1024.0) ** (j / (D // 4 - 1))
    t = np.arange(T, dtype=np.float64)
    theta = t[:, None] * freq[None, :]
    cos = np.cos(theta).astype(np.float32)
    sin = np.sin(theta).astype(np.float32)
    cos_sb = cos.reshape(NCPS, C, 32).transpose(1, 0, 2).reshape(C, NCPS * 32).copy()
    sin_sb = sin.reshape(NCPS, C, 32).transpose(1, 0, 2).reshape(C, NCPS * 32).copy()

    slopes = np.array([2.0 ** (-(h + 1) / 2.0) for h in range(H)], dtype=np.float64)
    r = np.exp(-slopes)
    i = np.arange(C, dtype=np.float64)
    qd = r[:, None] ** (i[None, :] + 1.0)
    kd = r[:, None] ** (C - 1.0 - i[None, :])
    rC = (r ** C).astype(np.float32)

    kd_tile = np.repeat(kd.T.astype(np.float32)[:, :, None], D, axis=2).reshape(C, H * D)
    qd_fm = np.broadcast_to(qd.astype(np.float32).reshape(1, H * C), (D, H * C)).copy()
    diff = i[None, :] - i[:, None]
    m = (diff >= 0)
    dm = np.where(m[None], r[:, None, None] ** np.where(m, diff, 0.0)[None], 0.0)
    dmaskT = dm.transpose(1, 0, 2).reshape(C, H * C).astype(bfdt)

    rc_tile = np.broadcast_to(np.repeat(rC, D).reshape(1, H * D), (D, H * D)).copy()
    magic = np.broadcast_to(np.array([[MAGIC, -MAGIC]], np.float32), (128, 2)).copy()
    return dict(
        c_rc=rc_tile.astype(np.float32),
        c_magic=magic,
        c_ident_bf=np.eye(C, dtype=bfdt),
        c_ident_f=np.eye(C, dtype=np.float32),
        c_ones_f=np.ones((C, 1), dtype=np.float32),
        c_cos=np.ascontiguousarray(cos_sb), c_sin=np.ascontiguousarray(sin_sb),
        c_kd=np.ascontiguousarray(kd_tile.astype(bfdt)),
        c_qd=qd_fm,
        c_dmask=np.ascontiguousarray(dmaskT),
    ), rC


def _chunks(cfg):
    out = [(s, ci) for s in range(NSEQ) for ci in range(NCPS)]
    return out[: cfg["nchunks"]]


class K:

    def __init__(self, cfg):
        self.cfg = cfg
        self.nc = bass.Bass("TRN2", target_bir_lowering=False)
        self.p = Prog()
        self.consts, self.rC = _host_constants()
        nc = self.nc

        self.x_d = nc.dram_tensor("x", [NSEQ, T, E], F32, kind="ExternalInput").ap()
        self.wq_d = nc.dram_tensor("wq", [E, E], F32, kind="ExternalInput").ap()
        self.wk_d = nc.dram_tensor("wk", [E, E], F32, kind="ExternalInput").ap()
        self.wv_d = nc.dram_tensor("wv", [E, E], F32, kind="ExternalInput").ap()
        self.wo_d = nc.dram_tensor("wo", [E, E], F32, kind="ExternalInput").ap()
        self.rwt_d = nc.dram_tensor("rwt", [128, 16], F32, kind="ExternalInput").ap()
        self.wfc_d = nc.dram_tensor("wfc", [HID, E], F32, kind="ExternalInput").ap()
        self.wpr_d = nc.dram_tensor("wpr", [E, HID], F32, kind="ExternalInput").ap()
        self.cst = {k: nc.dram_tensor(k, list(v.shape), mybir.dt.from_np(v.dtype),
                                      kind="ExternalInput").ap()
                    for k, v in self.consts.items()}

        skind = "ExternalOutput" if cfg["debug_scratch"] else "Internal"
        self.ln1f_d = nc.dram_tensor("s_ln1f", [NCH, C, E], BF16, kind=skind).ap()
        self.kfm_d = nc.dram_tensor("s_kfm", [NCH, C, E], BF16, kind=skind).ap()
        self.kdec_d = nc.dram_tensor("s_kdec", [NCH, C, E], BF16, kind=skind).ap()
        self.v_d = nc.dram_tensor("s_v", [NCH, C, E], BF16, kind=skind).ap()
        self.attn_d = nc.dram_tensor("s_attn", [NCH, C, E], BF16, kind=skind).ap()
        self.out1_d = nc.dram_tensor("s_out1", [NCH, C, E], F32, kind=skind).ap()
        self.ln2_d = nc.dram_tensor("s_ln2", [NCH, C, E], BF16, kind=skind).ap()
        self.h_d = nc.dram_tensor("s_h", [NCH, C, HID], BF16, kind=skind).ap()
        self.yp_d = nc.dram_tensor("s_yp", [NCH, C, E], F32, kind=skind).ap()
        self.w1q_d = nc.dram_tensor("s_w1q", [2, 32, 128, E], BF16, kind=skind).ap()
        self.w2q_d = nc.dram_tensor("s_w2q", [2, 32, 128, E], BF16, kind=skind).ap()
        self.out_d = nc.dram_tensor("out", [NSEQ, T, E], F32, kind="ExternalOutput").ap()
        if cfg["debug_scratch"]:
            self.z_dbg = nc.dram_tensor("s_z", [NCH, C, E], F32, kind="ExternalOutput").ap()

        self.P03 = nc.alloc_psum_tensor("P03", [128, 2048], F32)
        self.P47 = nc.alloc_psum_tensor("P47", [128, 2048], F32)

        nc_alloc = nc.alloc_sbuf_tensor
        self.ident_bf = nc_alloc("ident_bf", [C, C], BF16)
        self.ident_f = nc_alloc("ident_f", [C, C], F32)
        self.ones_f = nc_alloc("ones_f", [C, 1], F32)
        self.cos_sb = nc_alloc("cos_sb", [C, NCPS * 32], F32)
        self.sin_sb = nc_alloc("sin_sb", [C, NCPS * 32], F32)
        self.rwt_sb = nc_alloc("rwt_sb", [128, 16], F32)
        self.magic_sb = nc_alloc("magic_sb", [128, 2], F32)
        self.invr_sb = nc_alloc("invr_sb", [128, NCH], F32)
        self.invr2_sb = nc_alloc("invr2_sb", [128, NCH], F32)
        self.msum_sb = nc_alloc("msum_sb", [128, NCH], F32)
        self.arena_t = nc_alloc("arena", [128, 202 * 1024], U8)
        self.ar = Arena(self.arena_t[:])

        self.t_invr = {}
        self._stq = []
        self.t_invr2 = {}

        HH = HID // 2
        fc_jobs = {0: [], 1: []}
        for hf in range(2):
            for r in range(32):
                src = self.wfc_d[hf * HH + r * 128:hf * HH + (r + 1) * 128, :]
                fc_jobs[hf].append((src, self.w1q_d[hf][r]))
        pr_jobs = {0: [], 1: []}
        for kf in range(2):
            for r in range(16):
                for h2 in range(2):
                    src = self.wpr_d[r * 128:(r + 1) * 128,
                                     kf * HH + h2 * 2048:kf * HH + (h2 + 1) * 2048]
                    pr_jobs[kf].append((src, self.w2q_d[kf][h2 * 16 + r]))
        self.qjobs = {"P2b": fc_jobs[0], "P3h0": fc_jobs[1],
                      "P3h1": pr_jobs[0] + pr_jobs[1]}

        p = self.p
        self.TC = []
        for nm, tile in (("c_ident_bf", self.ident_bf), ("c_ident_f", self.ident_f),
                         ("c_ones_f", self.ones_f), ("c_cos", self.cos_sb),
                         ("c_sin", self.sin_sb)):
            self.TC.append(p.dma("sp", tile[:], self.cst[nm][:]))
        self.TC.append(p.dma("sp", self.rwt_sb[:], self.rwt_d[:]))
        self.TC.append(p.dma("sp", self.magic_sb[:], self.cst["c_magic"][:]))
        self.chunks = _chunks(cfg)

    def rstd(self, a_ap, a_tok, out_ap, scr):
        p = self.p
        t1 = p.op("act", lambda e: e.activation(scr[:, 0:1], a_ap, AF.Sqrt), after=[a_tok])
        t2 = p.op("dve", lambda e: e.reciprocal(scr[:, 1:2], scr[:, 0:1]), after=[t1])
        t3 = p.op("dve", lambda e: e.tensor_tensor(scr[:, 2:3], scr[:, 1:2], scr[:, 1:2], ALU.mult), after=[t2])
        t4 = p.op("dve", lambda e: e.tensor_tensor(scr[:, 3:4], scr[:, 2:3], a_ap, ALU.mult), after=[t3, a_tok])
        t5 = p.op("dve", lambda e: e.tensor_scalar(scr[:, 2:3], scr[:, 3:4], -0.5, 1.5, ALU.mult, ALU.add), after=[t4])
        t6 = p.op("dve", lambda e: e.tensor_tensor(out_ap, scr[:, 1:2], scr[:, 2:3], ALU.mult), after=[t5])
        return t6

    def ln_stats(self, x_buf, eps, scr, rstd_ap, mbias_ap, ncols=E, war=()):
        p = self.p
        nseg = ncols // 512
        t_bn = None
        for g in range(nseg):
            t_bn = p.op("dve", lambda e, g=g: e.bn_stats(scr["bn"][:, 6 * g:6 * (g + 1)],
                                                         x_buf.t[:, 512 * g:512 * (g + 1)]),
                        after=[x_buf.ready])
            x_buf.use(t_bn)
        t_mv = p.op("dve", lambda e, n=nseg: e.bn_aggr(scr["mv"][:], scr["bn"][:, 0:6 * n]), after=[t_bn])
        t_a = p.op("dve", lambda e: e.tensor_scalar(scr["s4"][:, 4:5], scr["mv"][:, 1:2], float(eps), None, ALU.add), after=[t_mv] + list(war))
        t_r = self.rstd(scr["s4"][:, 4:5], t_a, rstd_ap, scr["s4"][:, 0:4])
        t_m1 = p.op("dve", lambda e: e.tensor_tensor(scr["s4"][:, 5:6], scr["mv"][:, 0:1], rstd_ap, ALU.mult), after=[t_r, t_mv])
        t_mb = p.op("dve", lambda e: e.tensor_scalar(mbias_ap, scr["s4"][:, 5:6], -1.0, None, ALU.mult), after=[t_m1])
        return t_mb, t_r

    def pbank(self, b):
        pt = self.P03 if b < 4 else self.P47
        return pt[:, (b % 4) * 512:(b % 4 + 1) * 512]

    def mm_evac(self, b0, lhsT_fn, rhs_fn, nK, evac_fn, first_aft, gate=None):
        p = self.p
        ps = self.P03 if b0 < 4 else self.P47
        base = b0 % 4
        ev_toks = []
        mm_last = None
        for h in range(2):
            keys = (f"b{b0 + 2 * h}", f"b{b0 + 2 * h + 1}")
            t_mm = None
            for nn in (2 * h, 2 * h + 1):
                for kk in range(nK):
                    aft = []
                    if nn == 2 * h and kk == 0:
                        aft = self.psum_war(keys[0]) + self.psum_war(keys[1]) + list(first_aft)
                    if gate is not None:
                        aft = aft + list(gate(nn, kk))
                    t_mm = p.op("pe", lambda e, nn=nn, kk=kk: e.matmul(
                        ps[:, (base + nn) * 512:(base + nn + 1) * 512],
                        lhsT_fn(kk), rhs_fn(kk, nn),
                        start=(kk == 0), stop=(kk == nK - 1)),
                        after=aft, inc=(kk == nK - 1 and nn == 2 * h + 1))
            self.psum_set(keys[0], t_mm)
            self.psum_set(keys[1], t_mm)
            t_ev = evac_fn(h, t_mm, ps[:, (base + 2 * h) * 512:(base + 2 * h + 2) * 512])
            self.psum_use(keys[0], t_ev)
            self.psum_use(keys[1], t_ev)
            ev_toks.append(t_ev)
            mm_last = t_mm
        return ev_toks, mm_last

    def transpose16(self, src_fn, dst_buf, after, nblk=16, banks=(4, 5)):
        p = self.p
        toks = []
        for half in range((nblk + 7) // 8):
            bno = banks[half % len(banks)]
            bank = self.pbank(bno).bitcast(BF16)
            n_here = min(8, nblk - half * 8)
            t_last = None
            for jj in range(n_here):
                j = half * 8 + jj
                t_last = p.op("pe", lambda e, j=j, jj=jj, bank=bank: e.transpose(
                    bank[:, jj * 128:(jj + 1) * 128], src_fn(j), self.ident_bf[:]),
                    after=(list(after) + self.psum_war("b%d" % bno)) if jj == 0 else [],
                    inc=(jj == n_here - 1))
            self.psum_set("b%d" % bno, t_last)
            t_ev = p.op("act", lambda e, half=half, bank=bank, n=n_here: e.copy(
                dst_buf.t[:, half * 1024:half * 1024 + n * 128], bank[:, 0:n * 128]),
                after=[t_last] + (dst_buf.war() if half == 0 else []))
            self.psum_use("b%d" % bno, t_ev)
            toks.append(t_ev)
        dst_buf.set(toks[-1])
        return toks

    def phase2(self):
        p, ar = self.p, self.ar
        ar.reset()
        self.psum_init()
        wq_bf = [ar.ap([128, E], BF16) for _ in range(16)]
        stage = [ar.buf([128, E], F32)]
        qd_sb = ar.buf([D, H * C], F32)
        dm_sb = ar.buf([C, H * C], BF16)
        rc_sb = ar.buf([D, H * D], F32)
        kv32 = ar.buf([128, H * D], F32)
        kvbf = ar.buf([128, H * D], BF16)
        ln1f_t = [ar.buf([128, E], BF16) for _ in range(3)]
        kfm_t = [ar.buf([128, E], BF16) for _ in range(2)]
        kdec_t = [ar.buf([128, E], BF16) for _ in range(2)]
        v_t = [ar.buf([128, E], BF16) for _ in range(2)]
        qtm = ar.buf([128, E], F32)
        qbtm = ar.buf([128, E], BF16)
        qfm = ar.buf([128, E], BF16)
        qfmd = ar.buf([128, E], BF16)
        st = ar.buf([128, E], BF16)
        attnb = ar.buf([128, E], BF16)
        sq = ar.buf([128, E], BF16)
        sqs1 = ar.buf([128, 1024], F32)
        sqs2 = ar.buf([128, 512], F32)
        sqs3 = ar.buf([128, 256], F32)
        sqs4 = [ar.buf([128, 128], F32) for _ in range(2)]
        mrows = ar.buf([1, NCH * 128], F32)
        msall = ar.buf([128, NCH], F32)
        rot_t1 = ar.buf([128, 512], F32)
        rot_tmp = ar.buf([128, 512], F32)
        scr2 = ar.ap([128, 8], F32)
        scr2n = ar.ap([128, 4 * NCH], F32)
        msa_ap = ar.ap([128, 1], F32)

        chunks = self.chunks
        n = len(chunks)
        TC = self.TC

        def load_lf(gi):
            s, ci = chunks[gi]
            gc = s * NCPS + ci
            lf = ln1f_t[gi % 3]
            t = p.dma("sp", lf.t[:], self.ln1f_d[gc], after=lf.war())
            lf.set(t)

        def load_kv(gi):
            s, ci = chunks[gi]
            gc = s * NCPS + ci
            par = gi % 2
            for buf, src in ((kfm_t[par], self.kfm_d[gc]),
                             (kdec_t[par], self.kdec_d[gc]), (v_t[par], self.v_d[gc])):
                t = p.dma("sp", buf.t[:], src, after=buf.war())
                buf.set(t)

        load_lf(0)
        load_lf(1)
        load_kv(0)
        t_qd = p.dma("sp", qd_sb.t[:], self.cst["c_qd"][:])
        qd_sb.set(t_qd)
        t_rc = p.dma("sp", rc_sb.t[:], self.cst["c_rc"][:])
        rc_sb.set(t_rc)
        t_dm = p.dma("sp", dm_sb.t[:], self.cst["c_dmask"][:])
        dm_sb.set(t_dm)
        t_wq = self.load_weight_bf16(self.wq_d, wq_bf, stage)

        stat_pend = []

        def emit_stats():
            for gc0, t_s40, s4b in stat_pend:
                t_msm = p.op("pe", lambda e, s4b=s4b: e.matmul(
                    self.P03[0:1, 1024:1152], self.ones_f[:], s4b.t[:], start=True, stop=True),
                    after=self.psum_war("b2") + [t_s40] + TC)
                self.psum_set("b2", t_msm)
                s4b.use(t_msm)
                t_mre = p.op("dve", lambda e, gc0=gc0: e.tensor_copy(
                    mrows.t[0:1, gc0 * 128:(gc0 + 1) * 128], self.P03[0:1, 1024:1152]),
                    after=[t_msm])
                self.psum_use("b2", t_mre)
                mrows.use(t_mre)
                self.t_invr[gc0] = t_mre
            del stat_pend[:]

        def finish_stats():
            t_tr = None
            for i in range(NCH):
                t_tr = p.op("pe", lambda e, i=i: e.transpose(
                    self.P03[:, 1024 + i:1024 + i + 1],
                    mrows.t[0:1, i * 128:(i + 1) * 128], self.ident_f[0:1, 0:1]),
                    after=(self.psum_war("b2") + list(mrows.users) + TC) if i == 0 else [],
                    inc=(i == NCH - 1))
            self.psum_set("b2", t_tr)
            t_all = p.op("dve", lambda e: e.tensor_copy(msall.t[:], self.P03[:, 1024:1024 + NCH]),
                         after=[t_tr] + msall.war())
            self.psum_use("b2", t_all)
            msall.set(t_all)
            t_a = p.op("dve", lambda e: e.tensor_scalar(
                msall.t[:], msall.t[:], 1.0 / E, RMS_EPS, ALU.mult, ALU.add), after=[t_all])
            iv = self.invr_sb
            t1 = p.op("act", lambda e: e.activation(scr2n[:, 0:NCH], msall.t[:], AF.Sqrt), after=[t_a])
            t2 = p.op("dve", lambda e: e.reciprocal(scr2n[:, NCH:2 * NCH], scr2n[:, 0:NCH]), after=[t1])
            t3 = p.op("dve", lambda e: e.tensor_tensor(scr2n[:, 2 * NCH:3 * NCH], scr2n[:, NCH:2 * NCH],
                                                       scr2n[:, NCH:2 * NCH], ALU.mult), after=[t2])
            t4 = p.op("dve", lambda e: e.tensor_tensor(scr2n[:, 3 * NCH:4 * NCH], scr2n[:, 2 * NCH:3 * NCH],
                                                       msall.t[:], ALU.mult), after=[t3])
            t5 = p.op("dve", lambda e: e.tensor_scalar(scr2n[:, 2 * NCH:3 * NCH], scr2n[:, 3 * NCH:4 * NCH],
                                                       -0.5, 1.5, ALU.mult, ALU.add), after=[t4])
            t6 = p.op("dve", lambda e: e.tensor_tensor(iv[:, 0:NCH], scr2n[:, NCH:2 * NCH],
                                                       scr2n[:, 2 * NCH:3 * NCH], ALU.mult), after=[t5])
            msall.use(t6)
            for gc0 in range(NCH):
                self.t_invr[gc0] = t6

        def attention_scores(gi):
            par = gi % 2
            kfb = kfm_t[par]
            t_sc = None
            for h in range(H):
                aft = []
                if h == 0:
                    for b in (4, 5, 6, 7):
                        aft += self.psum_war("b%d" % b)
                    aft += [kfb.ready, qfm.ready]
                t_sc = p.op("pe", lambda e, h=h, kfb=kfb: e.matmul(
                    self.P47[:, h * 128:(h + 1) * 128],
                    kfb.t[:, h * 128:(h + 1) * 128],
                    qfm.t[:, h * 128:(h + 1) * 128],
                    start=True, stop=True), after=aft, inc=(h == H - 1))
            for b in (4, 5, 6, 7):
                self.psum_set("b%d" % b, t_sc)
            kfb.use(t_sc)
            qfm.use(t_sc)
            t_st = p.op("dve", lambda e: e.tensor_tensor(
                st.t[:], self.P47[:], dm_sb.t[:], ALU.mult),
                after=[t_sc, dm_sb.ready] + st.war())
            for b in (4, 5, 6, 7):
                self.psum_use("b%d" % b, t_st)
            st.set(t_st)
            return t_st

        def attention(gi, t_st):
            s, ci = chunks[gi]
            gc = s * NCPS + ci
            par = gi % 2
            first = (ci == 0)
            kfb, kdb, vb = kfm_t[par], kdec_t[par], v_t[par]

            t_u = None
            for h in range(H):
                aft = []
                if h == 0:
                    for b in (0, 1, 2, 3):
                        aft += self.psum_war("b%d" % b)
                    aft += [kdb.ready, vb.ready]
                t_u = p.op("pe", lambda e, h=h, kdb=kdb, vb=vb: e.matmul(
                    self.P03[:, h * 128:(h + 1) * 128],
                    kdb.t[:, h * 128:(h + 1) * 128],
                    vb.t[:, h * 128:(h + 1) * 128],
                    start=True, stop=True), after=aft, inc=(h == H - 1))
            for b in (0, 1, 2, 3):
                self.psum_set("b%d" % b, t_u)
            kdb.use(t_u)
            vb.use(t_u)

            t_a = None
            for h in range(H):
                aft = []
                if h == 0:
                    for b in (4, 5, 6, 7):
                        aft += self.psum_war("b%d" % b)
                    aft += [t_st, vb.ready]
                    if not first:
                        aft.append(kvbf.ready)
                if not first:
                    t_a = p.op("pe", lambda e, h=h: e.matmul(
                        self.P47[:, h * 128:(h + 1) * 128],
                        kvbf.t[:, h * 128:(h + 1) * 128],
                        qfmd.t[:, h * 128:(h + 1) * 128],
                        start=True, stop=False), after=aft, inc=False)
                    aft = []
                t_a = p.op("pe", lambda e, h=h, first=first, vb=vb: e.matmul(
                    self.P47[:, h * 128:(h + 1) * 128],
                    vb.t[:, h * 128:(h + 1) * 128],
                    st.t[:, h * 128:(h + 1) * 128],
                    start=first, stop=True), after=aft, inc=(h == H - 1))
            for b in (4, 5, 6, 7):
                self.psum_set("b%d" % b, t_a)
            qfmd.use(t_a)
            vb.use(t_a)
            st.use(t_a)
            if not first:
                kvbf.use(t_a)
            t_ae = []
            for half in range(2):
                t_e = p.op("act", lambda e, half=half: e.copy(
                    attnb.t[:, half * 1024:(half + 1) * 1024],
                    self.P47[:, half * 1024:(half + 1) * 1024]),
                    after=[t_a] + (attnb.war() if half == 0 else []))
                self.psum_use("b%d" % (4 + 2 * half), t_e)
                self.psum_use("b%d" % (5 + 2 * half), t_e)
                t_ae.append(t_e)
            attnb.set(t_ae[-1])
            self.store(self.attn_d[gc], attnb.t[:], [t_ae[-1]], attnb)

            if first:
                t_up = p.op("dve", lambda e: e.tensor_scalar(
                    kv32.t[:], self.P03[:], 1.0, None, ALU.mult),
                    after=[t_u] + kv32.war())
            else:
                t_rcm = p.op("dve", lambda e: e.tensor_tensor(
                    kv32.t[:], kv32.t[:], rc_sb.t[:], ALU.mult),
                    after=[rc_sb.ready] + kv32.war())
                t_up = p.op("dve", lambda e: e.tensor_tensor(
                    kv32.t[:], kv32.t[:], self.P03[:], ALU.add),
                    after=[t_rcm, t_u])
            kv32.set(t_up)
            for b in (0, 1, 2, 3):
                self.psum_use("b%d" % b, t_up)
            t_cast = p.op("act", lambda e: e.copy(kvbf.t[:], kv32.t[:]),
                          after=[t_up] + kvbf.war())
            kv32.use(t_cast)
            kvbf.set(t_cast)

            t_sq = p.op("dve", lambda e: e.tensor_tensor(sq.t[:], attnb.t[:], attnb.t[:], ALU.mult),
                        after=[t_ae[-1]] + sq.war())
            attnb.use(t_sq)
            sq.set(t_sq)
            t_s1 = p.op("dve", lambda e: e.tensor_tensor(sqs1.t[:], sq.t[:, 0:1024], sq.t[:, 1024:2048], ALU.add),
                        after=[t_sq] + sqs1.war())
            sq.use(t_s1)
            sqs1.set(t_s1)
            t_s2 = p.op("dve", lambda e: e.tensor_tensor(sqs2.t[:], sqs1.t[:, 0:512], sqs1.t[:, 512:1024], ALU.add),
                        after=[t_s1] + sqs2.war())
            sqs1.use(t_s2)
            sqs2.set(t_s2)
            t_s3 = p.op("dve", lambda e: e.tensor_tensor(sqs3.t[:], sqs2.t[:, 0:256], sqs2.t[:, 256:512], ALU.add),
                        after=[t_s2] + sqs3.war())
            sqs2.use(t_s3)
            sqs3.set(t_s3)
            s4b = sqs4[par]
            t_s4 = p.op("dve", lambda e, s4b=s4b: e.tensor_tensor(s4b.t[:], sqs3.t[:, 0:128], sqs3.t[:, 128:256], ALU.add),
                        after=[t_s3] + s4b.war())
            sqs3.use(t_s4)
            s4b.set(t_s4)
            stat_pend.append((gc, t_s4, s4b))

        for gi, (s, ci) in enumerate(chunks):
            gc = s * NCPS + ci
            self.flush_stores()
            emit_stats()
            if gi + 2 < n:
                load_lf(gi + 2)
            lf = ln1f_t[gi % 3]

            t_st_prev = attention_scores(gi - 1) if gi > 0 else None

            def evac_q(h, t_mm, ps_half):
                return p.op("act", lambda e, h=h, ps_half=ps_half: e.copy(
                    qtm.t[:, h * 1024:(h + 1) * 1024], ps_half),
                    after=[t_mm] + (qtm.war() if h == 0 else []))

            evs_q, t_mmq = self.mm_evac(
                0, lambda kk, lf=lf: lf.t[:, kk * 128:(kk + 1) * 128],
                lambda kk, nn: wq_bf[kk][:, nn * 512:(nn + 1) * 512], 16, evac_q,
                [lf.ready], gate=lambda nn, kk: [t_wq[kk]])
            lf.use(t_mmq)
            qtm.set(evs_q[-1])

            t_rot0 = self.rotary(qtm, ci, rot_t1, rot_tmp, hs=0, he=8, tok=evs_q[0])
            t_qb0 = p.op("act", lambda e: e.copy(qbtm.t[:, 0:1024], qtm.t[:, 0:1024]),
                         after=[t_rot0] + qbtm.war())
            qtm.use(t_qb0)
            t_rot = self.rotary(qtm, ci, rot_t1, rot_tmp, hs=8, he=16, tok=evs_q[1])
            t_qb = p.op("act", lambda e: e.copy(qbtm.t[:, 1024:2048], qtm.t[:, 1024:2048]),
                        after=[t_rot])
            qtm.use(t_qb)
            qbtm.set(t_qb)

            if gi > 0:
                attention(gi - 1, t_st_prev)

            tts0 = self.transpose16(lambda j: qbtm.t[:, j * 128:(j + 1) * 128], qfm,
                                    [t_qb0] + qfm.war(), nblk=8, banks=(0,))
            tts = self.transpose16(lambda j: qbtm.t[:, (8 + j) * 128:(9 + j) * 128],
                                   View(qfm, qfm.t[:, 1024:2048]),
                                   [t_qb], nblk=8, banks=(1,))
            qbtm.use(tts[-1])
            t_qd2 = p.op("dve", lambda e: e.tensor_tensor(qfmd.t[:], qfm.t[:], qd_sb.t[:],
                                                          ALU.mult),
                         after=[qfm.ready, qd_sb.ready] + qfmd.war())
            qfm.use(t_qd2)
            qfmd.set(t_qd2)

            if gi + 1 < n:
                load_kv(gi + 1)
        t_st_last = attention_scores(n - 1)
        attention(n - 1, t_st_last)
        emit_stats()
        finish_stats()



    def phase2b(self):
        p, ar = self.p, self.ar
        ar.reset()
        self.psum_init()
        wo_bf = [ar.ap([128, E], BF16) for _ in range(16)]
        stage = [ar.buf([128, E], F32) for _ in range(2)]
        x_t = [ar.buf([128, E], F32) for _ in range(2)]
        at_t = [ar.buf([128, E], BF16) for _ in range(2)]
        z_t = [ar.buf([128, E], F32) for _ in range(2)]
        t_t = [ar.buf([128, E], F32) for _ in range(2)]
        o1 = [ar.buf([128, E], F32) for _ in range(2)]
        scrs = [dict(bn=ar.ap([128, 24], F32), mv=ar.ap([128, 2], F32),
                     s4=ar.ap([128, 8], F32)) for _ in range(2)]
        rms = [ar.buf([128, 2], F32) for _ in range(2)]
        qc = self.quant_alloc(ar, nstage=2)

        t_wo = self.load_weight_bf16(self.wo_d, wo_bf, stage,
                                     row_scale=lambda kk: self.rwt_sb[:, kk:kk + 1])

        chunks = self.chunks
        n = len(chunks)

        def load(gi):
            s, ci = chunks[gi]
            gc = s * NCPS + ci
            xb, ab = x_t[gi % 2], at_t[gi % 2]
            t_x = p.dma("sp", xb.t[:], self.x_d[s, ci * C:(ci + 1) * C, :], after=xb.war())
            xb.set(t_x)
            t_at = p.dma("sp", ab.t[:], self.attn_d[gc], after=ab.war())
            ab.set(t_at)

        load(0)
        for gi, (s, ci) in enumerate(chunks):
            gc = s * NCPS + ci
            par = gi % 2
            self.flush_stores()
            if gi + 1 < n:
                load(gi + 1)
            xb, ab, zb, tb, ob = x_t[par], at_t[par], z_t[par], t_t[par], o1[par]
            b0 = 0 if par == 0 else 4
            self.quant_step_a(qc, self.qjobs["P2b"], n=1)

            def evac(h, t_mm, ps_half, zb=zb, gc=gc):
                t_z = p.op("act", lambda e, h=h, ps_half=ps_half: e.activation(
                    zb.t[:, h * 1024:(h + 1) * 1024], ps_half, AF.Identity,
                    scale=self.invr_sb[:, gc:gc + 1]),
                    after=[t_mm, self.t_invr.get(gc)] + (zb.war() if h == 0 else []))
                return t_z

            evs, t_mm = self.mm_evac(
                b0, lambda kk, ab=ab: ab.t[:, kk * 128:(kk + 1) * 128],
                lambda kk, nn: wo_bf[kk][:, nn * 512:(nn + 1) * 512], 16, evac,
                [ab.ready], gate=lambda nn, kk: [t_wo[kk]])
            ab.use(t_mm)
            zb.set(evs[-1])

            if self.cfg["debug_scratch"]:
                self.store(self.z_dbg[gc], zb.t[:], [evs[-1]], zb)
            t_mb, _ = self.ln_stats(zb, LN_EPS, scrs[par], rms[par].t[:, 0:1],
                                    rms[par].t[:, 1:2], war=rms[par].war())
            rms[par].set(t_mb)
            t_n = p.op("act", lambda e, zb=zb, par=par, tb=tb: e.activation(
                tb.t[:], zb.t[:], AF.Identity,
                bias=rms[par].t[:, 1:2], scale=rms[par].t[:, 0:1]),
                after=[rms[par].ready, zb.ready] + tb.war())
            rms[par].use(t_n)
            zb.use(t_n)
            tb.set(t_n)
            t_o1 = p.op("dve", lambda e, xb=xb, tb=tb, ob=ob: e.tensor_tensor(
                ob.t[:], tb.t[:], xb.t[:], ALU.add),
                after=[t_n, xb.ready] + ob.war())
            tb.use(t_o1)
            xb.use(t_o1)
            ob.set(t_o1)
            self.store(self.out1_d[gc], ob.t[:], [t_o1], ob)
            self.quant_step_b(qc, [(0, 1)] if par == 0 else [(4, 5)])
        self.quant_flush(qc, [(0, 1)])

    def store(self, dst, src, after, buf=None):
        self._stq.append((dst, src, list(after), buf))

    def flush_stores(self):
        for dst, src, after, buf in self._stq:
            tok = self.p.dma("pool", dst, src, after=after)
            if buf is not None:
                buf.use(tok)
        self._stq = []

    def psum_init(self):
        self._ps = {}

    def psum_war(self, key):
        b = self._ps.get(key)
        return b.war() if b else []

    def psum_set(self, key, tok):
        b = self._ps.setdefault(key, Buf(None))
        b.set(tok)

    def psum_use(self, key, tok):
        b = self._ps.setdefault(key, Buf(None))
        b.use(tok)

    def load_weight_bf16(self, w_dram, tiles, stage, row_scale=None, nk=16):
        p = self.p
        toks = []
        for kk in range(nk):
            st = stage[kk % len(stage)]
            t_ld = p.dma(("sp", "act")[kk % 2], st.t[:],
                         w_dram[kk * 128:(kk + 1) * 128, :], after=st.war())
            st.set(t_ld)
            dst = tiles[kk]
            if row_scale is None:
                t_c = p.op("act", lambda e, st=st, dst=dst: e.copy(dst, st.t[:]), after=[t_ld])
            else:
                t_c = p.op("act", lambda e, st=st, dst=dst, kk=kk: e.activation(
                    dst, st.t[:], AF.Identity, scale=row_scale(kk)), after=[t_ld])
            st.use(t_c)
            toks.append(t_c)
        return toks

    def rotary(self, buf, ci, rot_t1, rot_tmp, hs=0, he=H, tok=None):
        p = self.p
        nh = he - hs
        cosv = self.cos_sb[:, ci * 32:(ci + 1) * 32].unsqueeze(1).broadcast_to([C, nh, 32])
        sinv = self.sin_sb[:, ci * 32:(ci + 1) * 32].unsqueeze(1).broadcast_to([C, nh, 32])
        v3 = buf.t[:].rearrange("p (h d) -> p h d", h=H)[:, hs:he]
        x1v, x2v = v3[:, :, 0:32], v3[:, :, 64:96]
        t1v = rot_t1.t[:].rearrange("p (h d) -> p h d", h=H, d=32)[:, hs:he]
        tmpv = rot_tmp.t[:].rearrange("p (h d) -> p h d", h=H, d=32)[:, hs:he]
        t_ev = buf.ready if tok is None else tok
        t_c1 = p.op("dve", lambda e: e.tensor_copy(t1v, x1v), after=[t_ev] + rot_t1.war())
        buf.use(t_c1)
        rot_t1.set(t_c1)
        t_m1 = p.op("dve", lambda e, sv=sinv: e.tensor_tensor(tmpv, x2v, sv, ALU.mult),
                    after=[t_ev] + rot_tmp.war() + self.TC)
        buf.use(t_m1)
        rot_tmp.set(t_m1)
        t_m2 = p.op("dve", lambda e, cv=cosv: e.tensor_tensor(x1v, x1v, cv, ALU.mult), after=[t_c1, t_ev])
        t_m3 = p.op("dve", lambda e: e.tensor_tensor(x1v, x1v, tmpv, ALU.add), after=[t_m2, t_m1])
        rot_tmp.use(t_m3)
        t_m4 = p.op("dve", lambda e, cv=cosv: e.tensor_tensor(x2v, x2v, cv, ALU.mult), after=[t_m3])
        t_m5 = p.op("dve", lambda e, sv=sinv: e.tensor_tensor(tmpv, t1v, sv, ALU.mult),
                    after=[t_m4, t_c1] + rot_tmp.war())
        rot_t1.use(t_m5)
        rot_tmp.set(t_m5)
        t_rot = p.op("dve", lambda e: e.tensor_tensor(x2v, x2v, tmpv, ALU.subtract), after=[t_m5])
        rot_tmp.use(t_rot)
        buf.set(t_rot)
        return t_rot

    def phase1(self):
        p, ar = self.p, self.ar
        ar.reset()
        self.psum_init()
        wk_bf = [ar.ap([128, E], BF16) for _ in range(16)]
        wv_bf = [ar.ap([128, E], BF16) for _ in range(16)]
        stage = [ar.buf([128, E], F32)]
        kd_sb = ar.buf([C, H * D], BF16)
        x_t = [ar.buf([128, E], F32) for _ in range(2)]
        ln1 = ar.buf([128, E], BF16)
        ln1f = [ar.buf([128, E], BF16) for _ in range(3)]
        ktm = ar.buf([128, E], F32)
        kbtm = ar.buf([128, E], BF16)
        kfm = ar.buf([128, E], BF16)
        kdec = ar.buf([128, E], BF16)
        vbf = ar.buf([128, E], BF16)
        rot_t1 = ar.buf([128, 512], F32)
        rot_tmp = ar.buf([128, 512], F32)
        scrs = [dict(bn=ar.ap([128, 24], F32), mv=ar.ap([128, 2], F32),
                     s4=ar.ap([128, 8], F32)) for _ in range(2)]
        rms = [ar.buf([128, 2], F32) for _ in range(2)]

        chunks = self.chunks
        n = len(chunks)

        def load_x(gi):
            s, ci = chunks[gi]
            xb = x_t[gi % 2]
            t_x = p.dma("sp", xb.t[:], self.x_d[s, ci * C:(ci + 1) * C, :], after=xb.war())
            xb.set(t_x)

        def prep_ln(gi):
            par = gi % 2
            xb = x_t[par]
            t_mb, _ = self.ln_stats(xb, LN_EPS, scrs[par], rms[par].t[:, 0:1],
                                    rms[par].t[:, 1:2], war=rms[par].war())
            rms[par].set(t_mb)
            t_ln = p.op("act", lambda e, xb=xb, par=par: e.activation(
                ln1.t[:], xb.t[:], AF.Identity,
                bias=rms[par].t[:, 1:2], scale=rms[par].t[:, 0:1]),
                after=[t_mb, xb.ready] + ln1.war())
            rms[par].use(t_ln)
            xb.use(t_ln)
            ln1.set(t_ln)
            return t_ln

        def trans_ln(gi, t_ln):
            s, ci = chunks[gi]
            gc = s * NCPS + ci
            lf = ln1f[gi % 3]
            tts = self.transpose16(lambda j: ln1.t[:, j * 128:(j + 1) * 128], lf,
                                   [t_ln] + lf.war(), banks=(2, 3))
            ln1.use(tts[-1])
            self.store(self.ln1f_d[gc], lf.t[:], list(tts), lf)

        load_x(0)
        load_x(1)
        t_kd_c = p.dma("sp", kd_sb.t, self.cst["c_kd"][:])
        kd_sb.set(t_kd_c)
        t_wk = self.load_weight_bf16(self.wk_d, wk_bf, stage)
        t_wv = self.load_weight_bf16(self.wv_d, wv_bf, stage)
        t_ln0 = prep_ln(0)
        trans_ln(0, t_ln0)
        t_ln1 = prep_ln(1)
        trans_ln(1, t_ln1)

        for gi, (s, ci) in enumerate(chunks):
            gc = s * NCPS + ci
            par = gi % 2
            self.flush_stores()
            if gi + 2 < n:
                load_x(gi + 2)
            t_ln_next = prep_ln(gi + 2) if gi + 2 < n else None
            lf = ln1f[gi % 3]

            def evac_k(h, t_mm, ps_half):
                return p.op("act", lambda e, h=h, ps_half=ps_half: e.copy(
                    ktm.t[:, h * 1024:(h + 1) * 1024], ps_half),
                    after=[t_mm] + (ktm.war() if h == 0 else []))

            evs, t_mm = self.mm_evac(
                0, lambda kk, lf=lf: lf.t[:, kk * 128:(kk + 1) * 128],
                lambda kk, nn: wk_bf[kk][:, nn * 512:(nn + 1) * 512], 16, evac_k,
                [lf.ready], gate=lambda nn, kk: [t_wk[kk]])
            lf.use(t_mm)
            ktm.set(evs[-1])

            t_rot = self.rotary(ktm, ci, rot_t1, rot_tmp)

            def evac_v(h, t_mm2, ps_half):
                return p.op("act", lambda e, h=h, ps_half=ps_half: e.copy(
                    vbf.t[:, h * 1024:(h + 1) * 1024], ps_half),
                    after=[t_mm2] + (vbf.war() if h == 0 else []))

            evs_v, t_mmv = self.mm_evac(
                4, lambda kk, lf=lf: lf.t[:, kk * 128:(kk + 1) * 128],
                lambda kk, nn: wv_bf[kk][:, nn * 512:(nn + 1) * 512], 16, evac_v,
                [lf.ready], gate=lambda nn, kk: [t_wv[kk]])
            lf.use(t_mmv)
            vbf.set(evs_v[-1])
            self.store(self.v_d[gc], vbf.t[:], [evs_v[-1]], vbf)

            t_kb = p.op("act", lambda e: e.copy(kbtm.t[:], ktm.t[:]), after=[t_rot] + kbtm.war())
            ktm.use(t_kb)
            kbtm.set(t_kb)
            t_kd = p.op("dve", lambda e: e.tensor_tensor(kdec.t[:], ktm.t[:], kd_sb.t, ALU.mult),
                        after=[t_rot, kd_sb.ready] + kdec.war())
            ktm.use(t_kd)
            kdec.set(t_kd)
            self.store(self.kdec_d[gc], kdec.t[:], [t_kd], kdec)

            tts2 = self.transpose16(lambda j: kbtm.t[:, j * 128:(j + 1) * 128], kfm,
                                    [t_kb] + kfm.war(), banks=(0, 1))
            kbtm.use(tts2[-1])
            self.store(self.kfm_d[gc], kfm.t[:], tts2, kfm)

            if gi + 2 < n:
                trans_ln(gi + 2, t_ln_next)


    def quant_alloc(self, ar, nstage=1):
        return dict(
            stage=[ar.buf([128, E], F32) for _ in range(2)],
            wb=[ar.buf([128, E], BF16) for _ in range(2)],
            tq=[ar.buf([128, E], BF16) for _ in range(nstage)],
            tstage=ar.buf([128, E], BF16),
            gs=ar.ap([128, 32], F32),
            sc=ar.ap([128, 32], BF16),
            inv=ar.ap([128, 32], F32),
            rot=0,
            trot=0,
            loading=[],
            pend=[],
            mathed=[],
        )

    def quant_step_a(self, qc, jobs, n=1):
        p = self.p
        for pend in qc["pend"]:
            self.quant_math_dve(qc, pend)
            qc["mathed"].append(pend)
        qc["pend"] = []
        for pend in qc["loading"]:
            st, wb = pend["st"], pend["wb"]
            t_wb = p.op("act", lambda e, st=st, wb=wb: e.copy(wb.t[:], st.t[:]),
                        after=[st.ready] + wb.war())
            st.use(t_wb)
            wb.set(t_wb)
            qc["pend"].append(pend)
        qc["loading"] = []
        for _ in range(n):
            if jobs:
                qc["loading"].append(self.quant_load(qc, jobs.pop(0)))

    def quant_step_b(self, qc, bankpairs):
        for i, pend in enumerate(qc["mathed"]):
            self.quant_scatter(qc, pend, bankpairs[i % len(bankpairs)])
        qc["mathed"] = []

    def quant_flush(self, qc, bankpairs):
        while qc["pend"] or qc["loading"]:
            self.quant_step_a(qc, [], n=0)
            self.quant_step_b(qc, bankpairs)

    def quant_load(self, qc, job):
        p = self.p
        src_ap, dst_ap = job
        idx = qc["rot"] % 2
        qc["rot"] += 1
        st = qc["stage"][idx % len(qc["stage"])]
        wb = qc["wb"][idx]
        t_ld = p.dma("sp", st.t[:], src_ap, after=st.war())
        st.set(t_ld)
        return dict(dst=dst_ap, st=st, wb=wb)

    def quant_math_dve(self, qc, pend):
        p = self.p
        ng = E // 64
        wb = pend["wb"]
        tq = qc["tq"][qc["trot"] % len(qc["tq"])]
        qc["trot"] += 1
        w3 = wb.t[:].rearrange("p (g n) -> p g n", n=64)
        t_gs = p.op("dve", lambda e, w3=w3: e.tensor_reduce(
            qc["gs"][:, 0:ng], w3, AX.X, ALU.add, apply_absolute_value=True),
            after=[wb.ready])
        wb.use(t_gs)
        t_sc = p.op("dve", lambda e: e.tensor_scalar(
            qc["sc"][:, 0:ng], qc["gs"][:, 0:ng], 1.0 / 64, 1e-8, ALU.mult, ALU.max),
            after=[t_gs])
        t_iv = p.op("dve", lambda e: e.reciprocal(qc["inv"][:, 0:ng], qc["sc"][:, 0:ng]),
                    after=[t_sc])
        inv3 = qc["inv"][:, 0:ng].unsqueeze(2).broadcast_to([128, ng, 64])
        sc3 = qc["sc"][:, 0:ng].unsqueeze(2).broadcast_to([128, ng, 64])
        tq3 = tq.t[:].rearrange("p (g n) -> p g n", n=64)
        t_t = p.op("dve", lambda e, w3=w3, inv3=inv3, tq3=tq3: e.tensor_tensor(
            tq3, w3, inv3, ALU.mult), after=[t_iv, wb.ready] + tq.war())
        wb.use(t_t)
        t_c = p.op("dve", lambda e, tq=tq: e.tensor_scalar(
            tq.t[:], tq.t[:], 1.0, -1.0, ALU.min, ALU.max), after=[t_t])
        t_r = p.op("dve", lambda e, tq=tq: e.tensor_scalar(
            tq.t[:], tq.t[:], MAGIC, -MAGIC, ALU.add, ALU.add), after=[t_c])
        t_qs = p.op("dve", lambda e, tq3=tq3, sc3=sc3: e.tensor_tensor(
            tq3, tq3, sc3, ALU.mult), after=[t_r])
        tq.set(t_qs)
        pend["tq"] = tq

    def quant_scatter(self, qc, pend, banks):
        p = self.p
        tq, ts = pend["tq"], qc["tstage"]
        toks = []
        t_pe = None
        for grp in range(2):
            bno = banks[grp % len(banks)]
            bank = self.pbank(bno).bitcast(BF16)
            for jj in range(8):
                j = grp * 8 + jj
                t_pe = p.op("pe", lambda e, j=j, jj=jj, bank=bank, tq=tq: e.transpose(
                    bank[:, jj * 128:(jj + 1) * 128], tq.t[:, j * 128:(j + 1) * 128],
                    self.ident_bf[:]),
                    after=([tq.ready] + self.psum_war("b%d" % bno) + self.TC) if jj == 0 else [],
                    inc=(jj == 7))
            self.psum_set("b%d" % bno, t_pe)
            t_ev = p.op("act", lambda e, grp=grp, bank=bank: e.copy(
                ts.t[:, grp * 1024:(grp + 1) * 1024], bank[:]),
                after=[t_pe] + (ts.war() if grp == 0 else []))
            self.psum_use("b%d" % bno, t_ev)
            toks.append(t_ev)
        tq.use(t_pe)
        ts.set(toks[-1])
        t_st = p.dma("pool", pend["dst"], ts.t[:], after=toks)
        ts.use(t_st)

    def phase3(self, hf):
        p, ar = self.p, self.ar
        ar.reset()
        self.psum_init()
        HH = HID // 2
        w1_region = ar.ap([128, 16 * 8192], U8)
        w1 = [w1_region[:, i * 8192:(i + 1) * 8192].bitcast(BF16) for i in range(16)]
        h_bf = [ar.buf([128, HID // 2], BF16) for _ in range(1 if hf == 0 else 2)]
        ln2f = [ar.buf([128, E], BF16) for _ in range(2)]
        hscr = dict(bn=ar.ap([128, 48], F32), mv=ar.ap([128, 2], F32), s4=ar.ap([128, 8], F32))
        ms_scr = ar.ap([128, 2], F32)
        if hf == 0:
            o1_t = [ar.buf([128, E], F32) for _ in range(2)]
            ln2 = [ar.buf([128, E], BF16)]
            scrs = [dict(bn=ar.ap([128, 24], F32), mv=ar.ap([128, 2], F32),
                         s4=ar.ap([128, 8], F32)) for _ in range(2)]
            rms = [ar.buf([128, 2], F32) for _ in range(2)]
            qc = self.quant_alloc(ar, nstage=1)
            qjobs = self.qjobs["P3h0"]
            qn = 1
            qbank = [(2, 3)]
        else:
            qc = self.quant_alloc(ar, nstage=2)
            qjobs = self.qjobs["P3h1"]
            qn = 2
            qbank = [(2, 3), (6, 7)]

        chunks = self.chunks
        n = len(chunks)

        def load_o(gi):
            s, ci = chunks[gi]
            gc = s * NCPS + ci
            if hf == 0:
                ob = o1_t[gi % 2]
                t_o = p.dma("sp", ob.t[:], self.out1_d[gc], after=ob.war())
                ob.set(t_o)
            else:
                lf = ln2f[gi % 2]
                t_l2 = p.dma("sp", lf.t[:], self.ln2_d[gc], after=lf.war())
                lf.set(t_l2)

        def prep_ln(gi):
            par = gi % 2
            ob = o1_t[par]
            t_mb, _ = self.ln_stats(ob, LN_EPS, scrs[par], rms[par].t[:, 0:1],
                                    rms[par].t[:, 1:2], war=rms[par].war())
            rms[par].set(t_mb)
            lb = ln2[0]
            t_ln = p.op("act", lambda e, ob=ob, lb=lb, par=par: e.activation(
                lb.t[:], ob.t[:], AF.Identity,
                bias=rms[par].t[:, 1:2], scale=rms[par].t[:, 0:1]),
                after=[t_mb, ob.ready] + lb.war())
            rms[par].use(t_ln)
            ob.use(t_ln)
            lb.set(t_ln)
            return t_ln

        def trans_ln(gi, t_ln):
            s, ci = chunks[gi]
            gc = s * NCPS + ci
            par = gi % 2
            lf = ln2f[par]
            tts = self.transpose16(lambda j, lb=ln2[0]: lb.t[:, j * 128:(j + 1) * 128],
                                   lf, [t_ln] + lf.war(), banks=(0, 1))
            ln2[0].use(tts[-1])
            self.store(self.ln2_d[gc], lf.t[:], list(tts), lf)

        load_o(0)
        load_o(1)
        t_w1 = []
        for kk in range(16):
            wsrc = self.w1q_d[hf][:, :, kk * 128:(kk + 1) * 128]
            ta = p.dma("sp", w1[kk][:, 0:2048].rearrange("p (r c) -> p r c", c=128),
                       wsrc[0:16].rearrange("r p c -> p r c"))
            tb = p.dma("act", w1[kk][:, 2048:4096].rearrange("p (r c) -> p r c", c=128),
                       wsrc[16:32].rearrange("r p c -> p r c"))
            t_w1.append((ta, tb))
        if hf == 0:
            t_ln0 = prep_ln(0)
            trans_ln(0, t_ln0)

        for gi, (s, ci) in enumerate(chunks):
            gc = s * NCPS + ci
            par = gi % 2
            self.flush_stores()
            if hf == 0 and gi + 2 < n:
                load_o(gi + 2)
            if qc is not None:
                self.quant_step_a(qc, qjobs, n=qn)
            t_ln_next = None
            if hf == 0 and gi + 1 < n:
                t_ln_next = prep_ln(gi + 1)
            lf = ln2f[par]
            hb = h_bf[par % len(h_bf)]

            t_sq_toks = []
            t_bn_toks = []
            for slab in range(2):
                def evac(h, t_mm, ps_half, slab=slab, hb=hb):
                    return p.op("act", lambda e, h=h, ps_half=ps_half, slab=slab, hb=hb:
                                e.activation(
                                    hb.t[:, slab * 2048 + h * 1024:slab * 2048 + (h + 1) * 1024],
                                    ps_half, AF.Relu),
                                after=[t_mm] + (hb.war() if (slab == 0 and h == 0) else []))

                evs, t_mm = self.mm_evac(
                    0 if slab == 0 else 4,
                    lambda kk, lf=lf: lf.t[:, kk * 128:(kk + 1) * 128],
                    lambda kk, nn, slab=slab: w1[kk][:, slab * 2048 + nn * 512:
                                                     slab * 2048 + (nn + 1) * 512],
                    16, evac, [lf.ready], gate=lambda nn, kk: list(t_w1[kk]))
                lf.use(t_mm)
                hslab = hb.t[:, slab * 2048:(slab + 1) * 2048]
                t_sq = p.op("dve", lambda e, hslab=hslab: e.tensor_tensor(
                    hslab, hslab, hslab, ALU.mult), after=evs)
                t_sq_toks.append(t_sq)
                for g in range(4):
                    t_bn = p.op("dve", lambda e, slab=slab, g=g, hb=hb: e.bn_stats(
                        hscr["bn"][:, 6 * (slab * 4 + g):6 * (slab * 4 + g + 1)],
                        hb.t[:, slab * 2048 + 512 * g:slab * 2048 + 512 * (g + 1)]),
                        after=[t_sq])
                    t_bn_toks.append(t_bn)
            hb.set(t_sq_toks[-1])
            for t in t_bn_toks:
                hb.use(t)
            self.store(self.h_d[gc][:, hf * HH:(hf + 1) * HH], hb.t[:], t_sq_toks, hb)
            if hf == 1 and gi + 2 < n:
                load_o(gi + 2)
            if hf == 0 and gi + 1 < n:
                trans_ln(gi + 1, t_ln_next)
            if qc is not None:
                self.quant_step_b(qc, qbank)
            t_mv = p.op("dve", lambda e: e.bn_aggr(hscr["mv"][:], hscr["bn"][:, 0:48]),
                        after=[t_bn_toks[-1]])
            t_m2 = p.op("dve", lambda e: e.tensor_tensor(ms_scr[:, 0:1], hscr["mv"][:, 0:1],
                                                         hscr["mv"][:, 0:1], ALU.mult), after=[t_mv])
            if hf == 0:
                t_ms = p.op("dve", lambda e, gc=gc: e.tensor_tensor(
                    self.msum_sb[:, gc:gc + 1], ms_scr[:, 0:1], hscr["mv"][:, 1:2], ALU.add),
                    after=[t_m2])
                self.t_invr2[gc] = t_ms
            else:
                t_msb = p.op("dve", lambda e: e.tensor_tensor(
                    ms_scr[:, 1:2], ms_scr[:, 0:1], hscr["mv"][:, 1:2], ALU.add), after=[t_m2])
                t_tot = p.op("dve", lambda e, gc=gc: e.tensor_tensor(
                    ms_scr[:, 0:1], ms_scr[:, 1:2], self.msum_sb[:, gc:gc + 1], ALU.add),
                    after=[t_msb, self.t_invr2.get(gc)])
                t_a2 = p.op("dve", lambda e: e.tensor_scalar(
                    ms_scr[:, 1:2], ms_scr[:, 0:1], 0.5 / HH, FFN_EPS, ALU.mult, ALU.add),
                    after=[t_tot])
                t_ir2 = self.rstd(ms_scr[:, 1:2], t_a2, self.invr2_sb[:, gc:gc + 1], hscr["s4"][:, 0:4])
                self.t_invr2[gc] = t_ir2
        if qc is not None:
            self.quant_flush(qc, qbank)

    def phase4(self, kf):
        p, ar = self.p, self.ar
        ar.reset()
        self.psum_init()
        HH = HID // 2
        w2_region = ar.ap([128, 32 * 4096], U8)
        w2 = [w2_region[:, i * 4096:(i + 1) * 4096].bitcast(BF16) for i in range(32)]
        ypb = [ar.buf([128, E], F32) for _ in range(2)]
        o1b = [ar.buf([128, E], F32) for _ in range(2)]
        h_u8 = [ar.ap([128, 2 * HH], U8) for _ in range(2)]
        h_t = [Buf(h_u8[i]) for i in range(2)]
        h_bf = [h_u8[i].bitcast(BF16) for i in range(2)]
        h_f = [h_u8[i].bitcast(F32) for i in range(2)]
        hfm = [ar.buf([128, HH], BF16) for _ in range(3)]
        scrs = [dict(bn=ar.ap([128, 24], F32), mv=ar.ap([128, 2], F32),
                     s4=ar.ap([128, 8], F32)) for _ in range(2)]
        rms = [ar.buf([128, 2], F32) for _ in range(2)]

        chunks = self.chunks
        n = len(chunks)

        def load_h(gi):
            s, ci = chunks[gi]
            gc = s * NCPS + ci
            par = gi % 2
            t_h = p.dma("sp", h_bf[par], self.h_d[gc][:, kf * HH:(kf + 1) * HH],
                        after=h_t[par].war())
            h_t[par].set(t_h)

        def load_yo(gi):
            s, ci = chunks[gi]
            gc = s * NCPS + ci
            par = gi % 2
            t_yp = p.dma("sp", ypb[par].t[:], self.yp_d[gc], after=ypb[par].war())
            ypb[par].set(t_yp)
            t_o1 = p.dma("sp", o1b[par].t[:], self.out1_d[gc], after=o1b[par].war())
            o1b[par].set(t_o1)

        def load(gi):
            load_h(gi)
            if kf == 1:
                load_yo(gi)

        def htrans(gi):
            par = gi % 2
            fm = hfm[gi % 3]
            tts = self.transpose16(lambda j, par=par: h_bf[par][:, j * 128:(j + 1) * 128],
                                   fm, [h_t[par].ready] + fm.war(),
                                   nblk=32, banks=(4, 5, 6, 7))
            h_t[par].use(tts[-1])

        load(0)
        load(1)
        htrans_done = 0
        t_w2 = []
        for kk in range(32):
            h2, j = kk // 16, kk % 16
            wsrc = self.w2q_d[kf][h2 * 16:(h2 + 1) * 16, :, j * 128:(j + 1) * 128]
            t_w2.append(p.dma(("sp", "act")[kk % 2],
                              w2[kk][:].rearrange("p (r c) -> p r c", c=128),
                              wsrc.rearrange("r p c -> p r c")))
        htrans(0)
        htrans(1)
        for gi, (s, ci) in enumerate(chunks):
            gc = s * NCPS + ci
            par = gi % 2
            self.flush_stores()
            if gi + 2 < n:
                load_h(gi + 2)
            fm = hfm[gi % 3]

            if kf == 0:
                def evac(h, t_mm, ps_half, par=par, gc=gc):
                    return p.op("act", lambda e, h=h, ps_half=ps_half: e.activation(
                        ypb[par].t[:, h * 1024:(h + 1) * 1024], ps_half, AF.Identity,
                        scale=self.invr2_sb[:, gc:gc + 1]),
                        after=[t_mm, self.t_invr2.get(gc)] + (ypb[par].war() if h == 0 else []))

                evs, t_mm = self.mm_evac(
                    0, lambda kk, fm=fm: fm.t[:, kk * 128:(kk + 1) * 128],
                    lambda kk, nn: w2[kk][:, nn * 512:(nn + 1) * 512], 32, evac,
                    [fm.ready], gate=lambda nn, kk: [t_w2[kk]])
                fm.use(t_mm)
                ypb[par].set(evs[-1])
                if gi + 2 < n:
                    htrans(gi + 2)
                self.store(self.yp_d[gc], ypb[par].t[:], [evs[-1]], ypb[par])
            else:
                def evac(h, t_mm, ps_half, par=par, gc=gc):
                    return p.op("dve", lambda e, h=h, ps_half=ps_half, par=par, gc=gc:
                                e.scalar_tensor_tensor(
                                    ypb[par].t[:, h * 1024:(h + 1) * 1024], ps_half,
                                    self.invr2_sb[:, gc:gc + 1],
                                    ypb[par].t[:, h * 1024:(h + 1) * 1024],
                                    ALU.mult, ALU.add),
                                after=[t_mm, self.t_invr2.get(gc), ypb[par].ready])

                evs, t_mm = self.mm_evac(
                    0, lambda kk, fm=fm: fm.t[:, kk * 128:(kk + 1) * 128],
                    lambda kk, nn: w2[kk][:, nn * 512:(nn + 1) * 512], 32, evac,
                    [fm.ready], gate=lambda nn, kk: [t_w2[kk]])
                fm.use(t_mm)
                if gi + 2 < n:
                    htrans(gi + 2)
                ypb[par].set(evs[-1])
                t_mb, _ = self.ln_stats(ypb[par], LN_EPS, scrs[par], rms[par].t[:, 0:1],
                                        rms[par].t[:, 1:2], war=rms[par].war())
                rms[par].set(t_mb)
                t_n = p.op("act", lambda e, par=par: e.activation(
                    ypb[par].t[:], ypb[par].t[:], AF.Identity,
                    bias=rms[par].t[:, 1:2], scale=rms[par].t[:, 0:1]),
                    after=[t_mb, ypb[par].ready])
                rms[par].use(t_n)
                ypb[par].set(t_n)
                t_out = p.op("dve", lambda e, par=par: e.tensor_tensor(
                    ypb[par].t[:], ypb[par].t[:], o1b[par].t[:], ALU.add),
                    after=[t_n, o1b[par].ready])
                o1b[par].use(t_out)
                ypb[par].set(t_out)
                self.store(self.out_d[s, ci * C:(ci + 1) * C, :], ypb[par].t[:],
                           [t_out], ypb[par])
                if gi + 2 < n:
                    self.flush_stores()
                    load_yo(gi + 2)

    def finish(self):
        p = self.p
        if self.cfg["debug_scratch"]:
            dbg = self.nc.dram_tensor("s_dbg", [128, 2 * NCH], F32, kind="ExternalOutput").ap()
            p.dma("sp", dbg[:, 0:NCH], self.invr_sb[:],
                  after=[(s, p.cnt[s]) for s in ("dve", "act", "pe")])
            p.dma("sp", dbg[:, NCH:2 * NCH], self.invr2_sb[:],
                  after=[(s, p.cnt[s]) for s in ("dve", "act", "pe")])
        p.fence("sp", [(s, p.cnt[s]) for s in SEM_NAMES if s.startswith("d_")])
        es = contextlib.ExitStack()
        sems = {s: es.enter_context(self.nc.semaphore(s)) for s in SEM_NAMES}
        with es:
            p.emit(self.nc, sems)
        return self.nc


def build_nc(cfg):
    k = K(cfg)
    phases = []
    if "P1" in cfg["phases"]:
        phases.append(k.phase1)
    if "P2" in cfg["phases"]:
        phases.append(k.phase2)
    if "P2b" in cfg["phases"]:
        phases.append(k.phase2b)
    if "P3" in cfg["phases"]:
        phases.append(lambda: k.phase3(0))
        phases.append(lambda: k.phase3(1))
    if "P4" in cfg["phases"]:
        phases.append(lambda: k.phase4(0))
        phases.append(lambda: k.phase4(1))
    for ph in phases:
        ph()
        k.flush_stores()
        k.p.barrier()
    return k.finish(), k.consts


def _prep_inputs(inputs):
    consts, _ = _host_constants()
    rwt = np.ascontiguousarray(inputs["L0_attn_rms_w"].reshape(16, 128).T)
    base = dict(
        wq=np.ascontiguousarray(inputs["L0_attn_Wq"]),
        wk=np.ascontiguousarray(inputs["L0_attn_Wk"]),
        wv=np.ascontiguousarray(inputs["L0_attn_Wv"]),
        wo=np.ascontiguousarray(inputs["L0_attn_Wo"]),
        rwt=rwt,
        wfc=np.ascontiguousarray(inputs["L0_ffn_W_fc"]),
        wpr=np.ascontiguousarray(inputs["L0_ffn_W_proj"]),
    )
    base.update(consts)
    return base


def kernel(**inputs):
    x = np.ascontiguousarray(inputs["L0_attn_x"])
    cores = CFG["cores"] if CFG["cores"] is not None else list(range(8))
    base = _prep_inputs(inputs)
    nc, _ = build_nc(CFG)
    in_maps = []
    for i in range(len(cores)):
        m = dict(base)
        m["x"] = np.ascontiguousarray(x[i * NSEQ:(i + 1) * NSEQ])
        in_maps.append(m)
    res = run_bass_kernel_spmd(nc, in_maps, core_ids=cores)
    return np.concatenate([r["out"] for r in res.results], axis=0)


# version 38
# speedup vs baseline: 1.0682x; 1.0682x over previous
import contextlib
import math

import numpy as np
import ml_dtypes

import concourse.bass as bass
import concourse.mybir as mybir
from concourse.bass_utils import run_bass_kernel_spmd

F32 = mybir.dt.float32
BF16 = mybir.dt.bfloat16
U8 = mybir.dt.uint8
AF = mybir.ActivationFunctionType
ALU = mybir.AluOpType
AX = mybir.AxisListType
bfdt = ml_dtypes.bfloat16

E = 2048
H = 16
D = 128
C = 128
T = 2048
HID = 8192
NSEQ = 2
NCPS = T // C
NCH = NSEQ * NCPS
LN_EPS = 1e-5
RMS_EPS = 1e-5
FFN_EPS = float(np.finfo(np.float32).eps)
MAGIC = 12582912.0

CFG = dict(
    phases=("P1", "P2", "P2b", "P3", "P4"),
    debug_scratch=False,
    nchunks=NCH,
    cores=None,
)

ENGS = ("pe", "dve", "act", "pool", "sp")
NDMASEM = 4
DMA_QUEUES = ("sp", "act", "pool")
SEM_NAMES = ("pe", "dve", "act", "pool") + tuple(
    f"d_{q}{i}" for q in DMA_QUEUES for i in range(NDMASEM))
DSIZE = {F32: 4, BF16: 2, U8: 1}


class Prog:
    def __init__(self):
        self.stream = {e: [] for e in ENGS}
        self.cnt = {s: 0 for s in SEM_NAMES}
        self.hwm = {}
        self.dma_rot = {q: 0 for q in DMA_QUEUES}

    def _waits(self, eng, after):
        out = []
        for tok in after:
            if tok is None:
                continue
            sem, val = tok
            if val <= 0:
                continue
            k = (eng, sem)
            if self.hwm.get(k, 0) >= val:
                continue
            self.hwm[k] = val
            out.append((sem, val))
        return out

    def op(self, eng, fn, after=(), inc=True):
        if eng in ("dve", "act", "pool") and self.cnt[eng] > 0:
            after = list(after) + [(eng, self.cnt[eng])]
        waits = self._waits(eng, after)
        rec_inc = None
        tok = None
        if inc:
            self.cnt[eng] += 1
            tok = (eng, self.cnt[eng])
            rec_inc = (eng, 1)
        self.stream[eng].append((fn, waits, rec_inc))
        return tok

    def dma(self, queue, dst, src, after=()):
        idx = self.dma_rot[queue]
        self.dma_rot[queue] = (idx + 1) % NDMASEM
        sem = f"d_{queue}{idx}"
        after = list(after) + [(sem, self.cnt[sem])]
        waits = self._waits(queue, after)
        self.cnt[sem] += 16
        tok = (sem, self.cnt[sem])
        self.stream[queue].append(
            (lambda eng, d=dst, s=src: eng.dma_start(d, s), waits, (sem, 16))
        )
        return tok

    def fence(self, eng, after):
        waits = self._waits(eng, after)
        if waits:
            self.stream[eng].append((None, waits, None))

    def barrier(self):
        toks = [(s, self.cnt[s]) for s in SEM_NAMES]
        for e in ENGS:
            self.fence(e, toks)

    def emit(self, nc, sems):
        with nc.Block() as blk:
            decorators = {
                "pe": blk.tensor, "dve": blk.vector, "act": blk.scalar,
                "pool": blk.gpsimd, "sp": blk.sync,
            }
            for engname in ENGS:
                items = self.stream[engname]
                if not items:
                    continue

                def body(eng, items=items):
                    for fn, waits, inc in items:
                        for (sem, val) in waits:
                            eng.wait_ge(sems[sem], val)
                        if fn is None:
                            continue
                        inst = fn(eng)
                        if inc is not None:
                            sem, amount = inc
                            inst.then_inc(sems[sem], amount)

                decorators[engname](body)


class Buf:
    def __init__(self, ap):
        self.t = ap
        self.ready = None
        self.users = []

    def war(self):
        toks = list(self.users)
        if self.ready is not None:
            toks.append(self.ready)
        return toks

    def set(self, tok):
        self.ready = tok
        self.users = []

    def use(self, tok):
        if tok is not None:
            self.users.append(tok)


class View:

    def __init__(self, buf, ap):
        self.b = buf
        self.t = ap

    @property
    def ready(self):
        return self.b.ready

    def war(self):
        return self.b.war()

    def set(self, tok):
        self.b.set(tok)

    def use(self, tok):
        self.b.use(tok)


class Arena:

    def __init__(self, tile, base=0):
        self.tile = tile
        self.limit = tile.shape[1]
        self.base = base
        self.off = base

    def reset(self):
        self.off = self.base

    def carve(self, nbytes):
        off = (self.off + 63) & ~63
        assert off + nbytes <= self.limit, f"arena overflow: {off}+{nbytes} > {self.limit}"
        self.off = off + nbytes
        return off

    def ap(self, shape, dtype):
        n = shape[1] * DSIZE[dtype]
        off = self.carve(n)
        v = self.tile[0:shape[0], off:off + n]
        if dtype != U8:
            v = v.bitcast(dtype)
        return v

    def buf(self, shape, dtype):
        return Buf(self.ap(shape, dtype))


def _host_constants():
    j = np.arange(D // 4, dtype=np.float64)
    freq = (1.0 / 1024.0) ** (j / (D // 4 - 1))
    t = np.arange(T, dtype=np.float64)
    theta = t[:, None] * freq[None, :]
    cos = np.cos(theta).astype(np.float32)
    sin = np.sin(theta).astype(np.float32)
    cos_sb = cos.reshape(NCPS, C, 32).transpose(1, 0, 2).reshape(C, NCPS * 32).copy()
    sin_sb = sin.reshape(NCPS, C, 32).transpose(1, 0, 2).reshape(C, NCPS * 32).copy()

    slopes = np.array([2.0 ** (-(h + 1) / 2.0) for h in range(H)], dtype=np.float64)
    r = np.exp(-slopes)
    i = np.arange(C, dtype=np.float64)
    qd = r[:, None] ** (i[None, :] + 1.0)
    kd = r[:, None] ** (C - 1.0 - i[None, :])
    rC = (r ** C).astype(np.float32)

    kd_tile = np.repeat(kd.T.astype(np.float32)[:, :, None], D, axis=2).reshape(C, H * D)
    qd_fm = np.broadcast_to(qd.astype(np.float32).reshape(1, H * C), (D, H * C)).copy()
    diff = i[None, :] - i[:, None]
    m = (diff >= 0)
    dm = np.where(m[None], r[:, None, None] ** np.where(m, diff, 0.0)[None], 0.0)
    dmaskT = dm.transpose(1, 0, 2).reshape(C, H * C).astype(bfdt)

    rc_tile = np.broadcast_to(np.repeat(rC, D).reshape(1, H * D), (D, H * D)).copy()
    magic = np.broadcast_to(np.array([[MAGIC, -MAGIC]], np.float32), (128, 2)).copy()
    return dict(
        c_rc=rc_tile.astype(np.float32),
        c_magic=magic,
        c_ident_bf=np.eye(C, dtype=bfdt),
        c_ident_f=np.eye(C, dtype=np.float32),
        c_ones_f=np.ones((C, 1), dtype=np.float32),
        c_cos=np.ascontiguousarray(cos_sb), c_sin=np.ascontiguousarray(sin_sb),
        c_kd=np.ascontiguousarray(kd_tile.astype(bfdt)),
        c_qd=qd_fm,
        c_dmask=np.ascontiguousarray(dmaskT),
    ), rC


def _chunks(cfg):
    out = [(s, ci) for s in range(NSEQ) for ci in range(NCPS)]
    return out[: cfg["nchunks"]]


class K:

    def __init__(self, cfg):
        self.cfg = cfg
        self.nc = bass.Bass("TRN2", target_bir_lowering=False)
        self.p = Prog()
        self.consts, self.rC = _host_constants()
        nc = self.nc

        self.x_d = nc.dram_tensor("x", [NSEQ, T, E], F32, kind="ExternalInput").ap()
        self.wq_d = nc.dram_tensor("wq", [E, E], F32, kind="ExternalInput").ap()
        self.wk_d = nc.dram_tensor("wk", [E, E], F32, kind="ExternalInput").ap()
        self.wv_d = nc.dram_tensor("wv", [E, E], F32, kind="ExternalInput").ap()
        self.wo_d = nc.dram_tensor("wo", [E, E], F32, kind="ExternalInput").ap()
        self.rwt_d = nc.dram_tensor("rwt", [128, 16], F32, kind="ExternalInput").ap()
        self.wfc_d = nc.dram_tensor("wfc", [HID, E], F32, kind="ExternalInput").ap()
        self.wpr_d = nc.dram_tensor("wpr", [E, HID], F32, kind="ExternalInput").ap()
        self.cst = {k: nc.dram_tensor(k, list(v.shape), mybir.dt.from_np(v.dtype),
                                      kind="ExternalInput").ap()
                    for k, v in self.consts.items()}

        skind = "ExternalOutput" if cfg["debug_scratch"] else "Internal"
        self.ln1f_d = nc.dram_tensor("s_ln1f", [NCH, C, E], BF16, kind=skind).ap()
        self.kfm_d = nc.dram_tensor("s_kfm", [NCH, C, E], BF16, kind=skind).ap()
        self.kdec_d = nc.dram_tensor("s_kdec", [NCH, C, E], BF16, kind=skind).ap()
        self.v_d = nc.dram_tensor("s_v", [NCH, C, E], BF16, kind=skind).ap()
        self.attn_d = nc.dram_tensor("s_attn", [NCH, C, E], BF16, kind=skind).ap()
        self.out1_d = nc.dram_tensor("s_out1", [NCH, C, E], F32, kind=skind).ap()
        self.ln2_d = nc.dram_tensor("s_ln2", [NCH, C, E], BF16, kind=skind).ap()
        self.h_d = nc.dram_tensor("s_h", [NCH, C, HID], BF16, kind=skind).ap()
        self.yp_d = nc.dram_tensor("s_yp", [NCH, C, E], F32, kind=skind).ap()
        self.wqb_d = nc.dram_tensor("s_wqb", [16, 128, E], BF16, kind=skind).ap()
        self.wob_d = nc.dram_tensor("s_wob", [16, 128, E], BF16, kind=skind).ap()
        self.w1q_d = nc.dram_tensor("s_w1q", [2, 32, 128, E], BF16, kind=skind).ap()
        self.w2q_d = nc.dram_tensor("s_w2q", [2, 32, 128, E], BF16, kind=skind).ap()
        self.out_d = nc.dram_tensor("out", [NSEQ, T, E], F32, kind="ExternalOutput").ap()
        if cfg["debug_scratch"]:
            self.z_dbg = nc.dram_tensor("s_z", [NCH, C, E], F32, kind="ExternalOutput").ap()

        self.P03 = nc.alloc_psum_tensor("P03", [128, 2048], F32)
        self.P47 = nc.alloc_psum_tensor("P47", [128, 2048], F32)

        nc_alloc = nc.alloc_sbuf_tensor
        self.ident_bf = nc_alloc("ident_bf", [C, C], BF16)
        self.ident_f = nc_alloc("ident_f", [C, C], F32)
        self.ones_f = nc_alloc("ones_f", [C, 1], F32)
        self.cos_sb = nc_alloc("cos_sb", [C, NCPS * 32], F32)
        self.sin_sb = nc_alloc("sin_sb", [C, NCPS * 32], F32)
        self.rwt_sb = nc_alloc("rwt_sb", [128, 16], F32)
        self.magic_sb = nc_alloc("magic_sb", [128, 2], F32)
        self.invr_sb = nc_alloc("invr_sb", [128, NCH], F32)
        self.invr2_sb = nc_alloc("invr2_sb", [128, NCH], F32)
        self.msum_sb = nc_alloc("msum_sb", [128, NCH], F32)
        self.arena_t = nc_alloc("arena", [128, 202 * 1024], U8)
        self.ar = Arena(self.arena_t[:])

        self.t_invr = {}
        self._stq = []
        self.t_invr2 = {}

        HH = HID // 2
        fc_jobs = {0: [], 1: []}
        for hf in range(2):
            for r in range(32):
                src = self.wfc_d[hf * HH + r * 128:hf * HH + (r + 1) * 128, :]
                fc_jobs[hf].append((src, self.w1q_d[hf][r]))
        pr_jobs = {0: [], 1: []}
        for kf in range(2):
            for r in range(16):
                for h2 in range(2):
                    src = self.wpr_d[r * 128:(r + 1) * 128,
                                     kf * HH + h2 * 2048:kf * HH + (h2 + 1) * 2048]
                    pr_jobs[kf].append((src, self.w2q_d[kf][h2 * 16 + r]))
        self.qjobs = {"P2b": fc_jobs[0], "P3h0": fc_jobs[1],
                      "P3h1": pr_jobs[0] + pr_jobs[1]}

        p = self.p
        self.TC = []
        for nm, tile in (("c_ident_bf", self.ident_bf), ("c_ident_f", self.ident_f),
                         ("c_ones_f", self.ones_f), ("c_cos", self.cos_sb),
                         ("c_sin", self.sin_sb)):
            self.TC.append(p.dma("sp", tile[:], self.cst[nm][:]))
        self.TC.append(p.dma("sp", self.rwt_sb[:], self.rwt_d[:]))
        self.TC.append(p.dma("sp", self.magic_sb[:], self.cst["c_magic"][:]))
        self.chunks = _chunks(cfg)

    def rstd(self, a_ap, a_tok, out_ap, scr):
        p = self.p
        t1 = p.op("act", lambda e: e.activation(scr[:, 0:1], a_ap, AF.Sqrt), after=[a_tok])
        t2 = p.op("dve", lambda e: e.reciprocal(scr[:, 1:2], scr[:, 0:1]), after=[t1])
        t3 = p.op("dve", lambda e: e.tensor_tensor(scr[:, 2:3], scr[:, 1:2], scr[:, 1:2], ALU.mult), after=[t2])
        t4 = p.op("dve", lambda e: e.tensor_tensor(scr[:, 3:4], scr[:, 2:3], a_ap, ALU.mult), after=[t3, a_tok])
        t5 = p.op("dve", lambda e: e.tensor_scalar(scr[:, 2:3], scr[:, 3:4], -0.5, 1.5, ALU.mult, ALU.add), after=[t4])
        t6 = p.op("dve", lambda e: e.tensor_tensor(out_ap, scr[:, 1:2], scr[:, 2:3], ALU.mult), after=[t5])
        return t6

    def ln_stats(self, x_buf, eps, scr, rstd_ap, mbias_ap, ncols=E, war=()):
        p = self.p
        nseg = ncols // 512
        t_bn = None
        for g in range(nseg):
            t_bn = p.op("dve", lambda e, g=g: e.bn_stats(scr["bn"][:, 6 * g:6 * (g + 1)],
                                                         x_buf.t[:, 512 * g:512 * (g + 1)]),
                        after=[x_buf.ready])
            x_buf.use(t_bn)
        t_mv = p.op("dve", lambda e, n=nseg: e.bn_aggr(scr["mv"][:], scr["bn"][:, 0:6 * n]), after=[t_bn])
        t_a = p.op("dve", lambda e: e.tensor_scalar(scr["s4"][:, 4:5], scr["mv"][:, 1:2], float(eps), None, ALU.add), after=[t_mv] + list(war))
        t_r = self.rstd(scr["s4"][:, 4:5], t_a, rstd_ap, scr["s4"][:, 0:4])
        t_m1 = p.op("dve", lambda e: e.tensor_tensor(scr["s4"][:, 5:6], scr["mv"][:, 0:1], rstd_ap, ALU.mult), after=[t_r, t_mv])
        t_mb = p.op("dve", lambda e: e.tensor_scalar(mbias_ap, scr["s4"][:, 5:6], -1.0, None, ALU.mult), after=[t_m1])
        return t_mb, t_r

    def pbank(self, b):
        pt = self.P03 if b < 4 else self.P47
        return pt[:, (b % 4) * 512:(b % 4 + 1) * 512]

    def mm_evac(self, b0, lhsT_fn, rhs_fn, nK, evac_fn, first_aft, gate=None):
        p = self.p
        ps = self.P03 if b0 < 4 else self.P47
        base = b0 % 4
        ev_toks = []
        mm_last = None
        for h in range(2):
            keys = (f"b{b0 + 2 * h}", f"b{b0 + 2 * h + 1}")
            t_mm = None
            for nn in (2 * h, 2 * h + 1):
                for kk in range(nK):
                    aft = []
                    if nn == 2 * h and kk == 0:
                        aft = self.psum_war(keys[0]) + self.psum_war(keys[1]) + list(first_aft)
                    if gate is not None:
                        aft = aft + list(gate(nn, kk))
                    t_mm = p.op("pe", lambda e, nn=nn, kk=kk: e.matmul(
                        ps[:, (base + nn) * 512:(base + nn + 1) * 512],
                        lhsT_fn(kk), rhs_fn(kk, nn),
                        start=(kk == 0), stop=(kk == nK - 1)),
                        after=aft, inc=(kk == nK - 1 and nn == 2 * h + 1))
            self.psum_set(keys[0], t_mm)
            self.psum_set(keys[1], t_mm)
            t_ev = evac_fn(h, t_mm, ps[:, (base + 2 * h) * 512:(base + 2 * h + 2) * 512])
            self.psum_use(keys[0], t_ev)
            self.psum_use(keys[1], t_ev)
            ev_toks.append(t_ev)
            mm_last = t_mm
        return ev_toks, mm_last

    def transpose16(self, src_fn, dst_buf, after, nblk=16, banks=(4, 5)):
        p = self.p
        toks = []
        for half in range((nblk + 7) // 8):
            bno = banks[half % len(banks)]
            bank = self.pbank(bno).bitcast(BF16)
            n_here = min(8, nblk - half * 8)
            t_last = None
            for jj in range(n_here):
                j = half * 8 + jj
                t_last = p.op("pe", lambda e, j=j, jj=jj, bank=bank: e.transpose(
                    bank[:, jj * 128:(jj + 1) * 128], src_fn(j), self.ident_bf[:]),
                    after=(list(after) + self.psum_war("b%d" % bno)) if jj == 0 else [],
                    inc=(jj == n_here - 1))
            self.psum_set("b%d" % bno, t_last)
            t_ev = p.op("act", lambda e, half=half, bank=bank, n=n_here: e.copy(
                dst_buf.t[:, half * 1024:half * 1024 + n * 128], bank[:, 0:n * 128]),
                after=[t_last] + (dst_buf.war() if half == 0 else []))
            self.psum_use("b%d" % bno, t_ev)
            toks.append(t_ev)
        dst_buf.set(toks[-1])
        return toks

    def phase2(self):
        p, ar = self.p, self.ar
        ar.reset()
        self.psum_init()
        wq_bf = [ar.ap([128, E], BF16) for _ in range(16)]
        stage = [ar.buf([128, E], F32)]
        qd_sb = ar.buf([D, H * C], F32)
        dm_sb = ar.buf([C, H * C], BF16)
        rc_sb = ar.buf([D, H * D], F32)
        kv32 = ar.buf([128, H * D], F32)
        kvbf = ar.buf([128, H * D], BF16)
        ln1f_t = [ar.buf([128, E], BF16) for _ in range(3)]
        kfm_t = [ar.buf([128, E], BF16) for _ in range(2)]
        kdec_t = [ar.buf([128, E], BF16) for _ in range(2)]
        v_t = [ar.buf([128, E], BF16) for _ in range(2)]
        qtm = ar.buf([128, E], F32)
        qbtm = ar.buf([128, E], BF16)
        qfm = ar.buf([128, E], BF16)
        qfmd = ar.buf([128, E], BF16)
        st = ar.buf([128, E], BF16)
        attnb = ar.buf([128, E], BF16)
        sq = st
        sqs1 = ar.buf([128, 1024], F32)
        sqs2 = ar.buf([128, 512], F32)
        sqs3 = ar.buf([128, 256], F32)
        sqs4 = [ar.buf([128, 128], F32) for _ in range(2)]
        mrows = ar.buf([1, NCH * 128], F32)
        msall = ar.buf([128, NCH], F32)
        rot_t1 = ar.buf([128, 512], F32)
        rot_tmp = ar.buf([128, 512], F32)
        scr2 = ar.ap([128, 8], F32)
        scr2n = ar.ap([128, 4 * NCH], F32)
        msa_ap = ar.ap([128, 1], F32)

        chunks = self.chunks
        n = len(chunks)
        TC = self.TC

        def load_lf(gi):
            s, ci = chunks[gi]
            gc = s * NCPS + ci
            lf = ln1f_t[gi % 3]
            t = p.dma("sp", lf.t[:], self.ln1f_d[gc], after=lf.war())
            lf.set(t)

        def load_kv(gi):
            s, ci = chunks[gi]
            gc = s * NCPS + ci
            par = gi % 2
            for buf, src in ((kfm_t[par], self.kfm_d[gc]),
                             (kdec_t[par], self.kdec_d[gc]), (v_t[par], self.v_d[gc])):
                t = p.dma("sp", buf.t[:], src, after=buf.war())
                buf.set(t)

        load_lf(0)
        load_lf(1)
        load_kv(0)
        t_qd = p.dma("sp", qd_sb.t[:], self.cst["c_qd"][:])
        qd_sb.set(t_qd)
        t_rc = p.dma("sp", rc_sb.t[:], self.cst["c_rc"][:])
        rc_sb.set(t_rc)
        t_dm = p.dma("sp", dm_sb.t[:], self.cst["c_dmask"][:])
        dm_sb.set(t_dm)
        t_wq = self.load_weight_bf16(self.wq_d, wq_bf, stage)
        wc = dict(stage=stage, wb=ar.buf([128, E], BF16), loading=[], k=0)

        stat_pend = []

        def emit_stats():
            for gc0, t_s40, s4b in stat_pend:
                t_msm = p.op("pe", lambda e, s4b=s4b: e.matmul(
                    self.P03[0:1, 1024:1152], self.ones_f[:], s4b.t[:], start=True, stop=True),
                    after=self.psum_war("b2") + [t_s40] + TC)
                self.psum_set("b2", t_msm)
                s4b.use(t_msm)
                t_mre = p.op("dve", lambda e, gc0=gc0: e.tensor_copy(
                    mrows.t[0:1, gc0 * 128:(gc0 + 1) * 128], self.P03[0:1, 1024:1152]),
                    after=[t_msm])
                self.psum_use("b2", t_mre)
                mrows.use(t_mre)
                self.t_invr[gc0] = t_mre
            del stat_pend[:]

        def finish_stats():
            t_tr = None
            for i in range(NCH):
                t_tr = p.op("pe", lambda e, i=i: e.transpose(
                    self.P03[:, 1024 + i:1024 + i + 1],
                    mrows.t[0:1, i * 128:(i + 1) * 128], self.ident_f[0:1, 0:1]),
                    after=(self.psum_war("b2") + list(mrows.users) + TC) if i == 0 else [],
                    inc=(i == NCH - 1))
            self.psum_set("b2", t_tr)
            t_all = p.op("dve", lambda e: e.tensor_copy(msall.t[:], self.P03[:, 1024:1024 + NCH]),
                         after=[t_tr] + msall.war())
            self.psum_use("b2", t_all)
            msall.set(t_all)
            t_a = p.op("dve", lambda e: e.tensor_scalar(
                msall.t[:], msall.t[:], 1.0 / E, RMS_EPS, ALU.mult, ALU.add), after=[t_all])
            iv = self.invr_sb
            t1 = p.op("act", lambda e: e.activation(scr2n[:, 0:NCH], msall.t[:], AF.Sqrt), after=[t_a])
            t2 = p.op("dve", lambda e: e.reciprocal(scr2n[:, NCH:2 * NCH], scr2n[:, 0:NCH]), after=[t1])
            t3 = p.op("dve", lambda e: e.tensor_tensor(scr2n[:, 2 * NCH:3 * NCH], scr2n[:, NCH:2 * NCH],
                                                       scr2n[:, NCH:2 * NCH], ALU.mult), after=[t2])
            t4 = p.op("dve", lambda e: e.tensor_tensor(scr2n[:, 3 * NCH:4 * NCH], scr2n[:, 2 * NCH:3 * NCH],
                                                       msall.t[:], ALU.mult), after=[t3])
            t5 = p.op("dve", lambda e: e.tensor_scalar(scr2n[:, 2 * NCH:3 * NCH], scr2n[:, 3 * NCH:4 * NCH],
                                                       -0.5, 1.5, ALU.mult, ALU.add), after=[t4])
            t6 = p.op("dve", lambda e: e.tensor_tensor(iv[:, 0:NCH], scr2n[:, NCH:2 * NCH],
                                                       scr2n[:, 2 * NCH:3 * NCH], ALU.mult), after=[t5])
            msall.use(t6)
            for gc0 in range(NCH):
                self.t_invr[gc0] = t6

        def attention_scores(gi):
            par = gi % 2
            kfb = kfm_t[par]
            t_sc = None
            for h in range(H):
                aft = []
                if h == 0:
                    for b in (4, 5, 6, 7):
                        aft += self.psum_war("b%d" % b)
                    aft += [kfb.ready, qfm.ready]
                t_sc = p.op("pe", lambda e, h=h, kfb=kfb: e.matmul(
                    self.P47[:, h * 128:(h + 1) * 128],
                    kfb.t[:, h * 128:(h + 1) * 128],
                    qfm.t[:, h * 128:(h + 1) * 128],
                    start=True, stop=True), after=aft, inc=(h == H - 1))
            for b in (4, 5, 6, 7):
                self.psum_set("b%d" % b, t_sc)
            kfb.use(t_sc)
            qfm.use(t_sc)
            t_st = p.op("dve", lambda e: e.tensor_tensor(
                st.t[:], self.P47[:], dm_sb.t[:], ALU.mult),
                after=[t_sc, dm_sb.ready] + st.war())
            for b in (4, 5, 6, 7):
                self.psum_use("b%d" % b, t_st)
            st.set(t_st)
            return t_st

        def attention(gi, t_st):
            s, ci = chunks[gi]
            gc = s * NCPS + ci
            par = gi % 2
            first = (ci == 0)
            kfb, kdb, vb = kfm_t[par], kdec_t[par], v_t[par]

            t_u = None
            for h in range(H):
                aft = []
                if h == 0:
                    for b in (0, 1, 2, 3):
                        aft += self.psum_war("b%d" % b)
                    aft += [kdb.ready, vb.ready]
                t_u = p.op("pe", lambda e, h=h, kdb=kdb, vb=vb: e.matmul(
                    self.P03[:, h * 128:(h + 1) * 128],
                    kdb.t[:, h * 128:(h + 1) * 128],
                    vb.t[:, h * 128:(h + 1) * 128],
                    start=True, stop=True), after=aft, inc=(h == H - 1))
            for b in (0, 1, 2, 3):
                self.psum_set("b%d" % b, t_u)
            kdb.use(t_u)
            vb.use(t_u)

            t_a = None
            for h in range(H):
                aft = []
                if h == 0:
                    for b in (4, 5, 6, 7):
                        aft += self.psum_war("b%d" % b)
                    aft += [t_st, vb.ready]
                    if not first:
                        aft.append(kvbf.ready)
                if not first:
                    t_a = p.op("pe", lambda e, h=h: e.matmul(
                        self.P47[:, h * 128:(h + 1) * 128],
                        kvbf.t[:, h * 128:(h + 1) * 128],
                        qfmd.t[:, h * 128:(h + 1) * 128],
                        start=True, stop=False), after=aft, inc=False)
                    aft = []
                t_a = p.op("pe", lambda e, h=h, first=first, vb=vb: e.matmul(
                    self.P47[:, h * 128:(h + 1) * 128],
                    vb.t[:, h * 128:(h + 1) * 128],
                    st.t[:, h * 128:(h + 1) * 128],
                    start=first, stop=True), after=aft, inc=(h == H - 1))
            for b in (4, 5, 6, 7):
                self.psum_set("b%d" % b, t_a)
            qfmd.use(t_a)
            vb.use(t_a)
            st.use(t_a)
            if not first:
                kvbf.use(t_a)
            t_ae = []
            for half in range(2):
                t_e = p.op("act", lambda e, half=half: e.copy(
                    attnb.t[:, half * 1024:(half + 1) * 1024],
                    self.P47[:, half * 1024:(half + 1) * 1024]),
                    after=[t_a] + (attnb.war() if half == 0 else []))
                self.psum_use("b%d" % (4 + 2 * half), t_e)
                self.psum_use("b%d" % (5 + 2 * half), t_e)
                t_ae.append(t_e)
            attnb.set(t_ae[-1])
            self.store(self.attn_d[gc], attnb.t[:], [t_ae[-1]], attnb)

            if first:
                t_up = p.op("dve", lambda e: e.tensor_scalar(
                    kv32.t[:], self.P03[:], 1.0, None, ALU.mult),
                    after=[t_u] + kv32.war())
            else:
                t_rcm = p.op("dve", lambda e: e.tensor_tensor(
                    kv32.t[:], kv32.t[:], rc_sb.t[:], ALU.mult),
                    after=[rc_sb.ready] + kv32.war())
                t_up = p.op("dve", lambda e: e.tensor_tensor(
                    kv32.t[:], kv32.t[:], self.P03[:], ALU.add),
                    after=[t_rcm, t_u])
            kv32.set(t_up)
            for b in (0, 1, 2, 3):
                self.psum_use("b%d" % b, t_up)
            t_cast = p.op("act", lambda e: e.copy(kvbf.t[:], kv32.t[:]),
                          after=[t_up] + kvbf.war())
            kv32.use(t_cast)
            kvbf.set(t_cast)

            t_sq = p.op("dve", lambda e: e.tensor_tensor(sq.t[:], attnb.t[:], attnb.t[:], ALU.mult),
                        after=[t_ae[-1]] + sq.war())
            attnb.use(t_sq)
            sq.set(t_sq)
            t_s1 = p.op("dve", lambda e: e.tensor_tensor(sqs1.t[:], sq.t[:, 0:1024], sq.t[:, 1024:2048], ALU.add),
                        after=[t_sq] + sqs1.war())
            sq.use(t_s1)
            sqs1.set(t_s1)
            t_s2 = p.op("dve", lambda e: e.tensor_tensor(sqs2.t[:], sqs1.t[:, 0:512], sqs1.t[:, 512:1024], ALU.add),
                        after=[t_s1] + sqs2.war())
            sqs1.use(t_s2)
            sqs2.set(t_s2)
            t_s3 = p.op("dve", lambda e: e.tensor_tensor(sqs3.t[:], sqs2.t[:, 0:256], sqs2.t[:, 256:512], ALU.add),
                        after=[t_s2] + sqs3.war())
            sqs2.use(t_s3)
            sqs3.set(t_s3)
            s4b = sqs4[par]
            t_s4 = p.op("dve", lambda e, s4b=s4b: e.tensor_tensor(s4b.t[:], sqs3.t[:, 0:128], sqs3.t[:, 128:256], ALU.add),
                        after=[t_s3] + s4b.war())
            sqs3.use(t_s4)
            s4b.set(t_s4)
            stat_pend.append((gc, t_s4, s4b))

        for gi, (s, ci) in enumerate(chunks):
            gc = s * NCPS + ci
            self.flush_stores()
            emit_stats()
            if gi + 2 < n:
                load_lf(gi + 2)
            lf = ln1f_t[gi % 3]

            t_st_prev = attention_scores(gi - 1) if gi > 0 else None
            if gi >= 8:
                self.wcast_step(wc, self.wo_d, self.wob_d,
                                row_scale=lambda kk: self.rwt_sb[:, kk:kk + 1])

            def evac_q(h, t_mm, ps_half):
                return p.op("act", lambda e, h=h, ps_half=ps_half: e.copy(
                    qtm.t[:, h * 1024:(h + 1) * 1024], ps_half),
                    after=[t_mm] + (qtm.war() if h == 0 else []))

            evs_q, t_mmq = self.mm_evac(
                0, lambda kk, lf=lf: lf.t[:, kk * 128:(kk + 1) * 128],
                lambda kk, nn: wq_bf[kk][:, nn * 512:(nn + 1) * 512], 16, evac_q,
                [lf.ready], gate=lambda nn, kk: [t_wq[kk]])
            lf.use(t_mmq)
            qtm.set(evs_q[-1])

            t_rot = self.rotary(qtm, ci, rot_t1, rot_tmp)
            t_qb = p.op("act", lambda e: e.copy(qbtm.t[:], qtm.t[:]),
                        after=[t_rot] + qbtm.war())
            qtm.use(t_qb)
            qbtm.set(t_qb)

            if gi > 0:
                attention(gi - 1, t_st_prev)

            tts = self.transpose16(lambda j: qbtm.t[:, j * 128:(j + 1) * 128], qfm,
                                   [t_qb] + qfm.war(), banks=(0, 1))
            qbtm.use(tts[-1])
            t_qd2 = p.op("dve", lambda e: e.tensor_tensor(qfmd.t[:], qfm.t[:], qd_sb.t[:],
                                                          ALU.mult),
                         after=[qfm.ready, qd_sb.ready] + qfmd.war())
            qfm.use(t_qd2)
            qfmd.set(t_qd2)

            if gi + 1 < n:
                load_kv(gi + 1)
        t_st_last = attention_scores(n - 1)
        attention(n - 1, t_st_last)
        emit_stats()
        finish_stats()



    def phase2b(self):
        p, ar = self.p, self.ar
        ar.reset()
        self.psum_init()
        wo_bf = [ar.ap([128, E], BF16) for _ in range(16)]
        stage = [ar.buf([128, E], F32) for _ in range(2)]
        x_t = [ar.buf([128, E], F32) for _ in range(2)]
        at_t = [ar.buf([128, E], BF16) for _ in range(2)]
        z_t = [ar.buf([128, E], F32) for _ in range(2)]
        t_t = [ar.buf([128, E], F32) for _ in range(2)]
        o1 = [ar.buf([128, E], F32) for _ in range(2)]
        scrs = [dict(bn=ar.ap([128, 24], F32), mv=ar.ap([128, 2], F32),
                     s4=ar.ap([128, 8], F32)) for _ in range(2)]
        rms = [ar.buf([128, 2], F32) for _ in range(2)]
        qc = self.quant_alloc(ar, nstage=2)

        t_wo = self.load_weight_pre(self.wob_d, wo_bf)

        chunks = self.chunks
        n = len(chunks)

        def load(gi):
            s, ci = chunks[gi]
            gc = s * NCPS + ci
            xb, ab = x_t[gi % 2], at_t[gi % 2]
            t_x = p.dma("sp", xb.t[:], self.x_d[s, ci * C:(ci + 1) * C, :], after=xb.war())
            xb.set(t_x)
            t_at = p.dma("sp", ab.t[:], self.attn_d[gc], after=ab.war())
            ab.set(t_at)

        load(0)
        for gi, (s, ci) in enumerate(chunks):
            gc = s * NCPS + ci
            par = gi % 2
            self.flush_stores()
            if gi + 1 < n:
                load(gi + 1)
            xb, ab, zb, tb, ob = x_t[par], at_t[par], z_t[par], t_t[par], o1[par]
            b0 = 0 if par == 0 else 4
            self.quant_step_a(qc, self.qjobs["P2b"], n=1)

            def evac(h, t_mm, ps_half, zb=zb, gc=gc):
                t_z = p.op("act", lambda e, h=h, ps_half=ps_half: e.activation(
                    zb.t[:, h * 1024:(h + 1) * 1024], ps_half, AF.Identity,
                    scale=self.invr_sb[:, gc:gc + 1]),
                    after=[t_mm, self.t_invr.get(gc)] + (zb.war() if h == 0 else []))
                return t_z

            evs, t_mm = self.mm_evac(
                b0, lambda kk, ab=ab: ab.t[:, kk * 128:(kk + 1) * 128],
                lambda kk, nn: wo_bf[kk][:, nn * 512:(nn + 1) * 512], 16, evac,
                [ab.ready], gate=lambda nn, kk: [t_wo[kk]])
            ab.use(t_mm)
            zb.set(evs[-1])

            if self.cfg["debug_scratch"]:
                self.store(self.z_dbg[gc], zb.t[:], [evs[-1]], zb)
            t_mb, _ = self.ln_stats(zb, LN_EPS, scrs[par], rms[par].t[:, 0:1],
                                    rms[par].t[:, 1:2], war=rms[par].war())
            rms[par].set(t_mb)
            t_n = p.op("act", lambda e, zb=zb, par=par, tb=tb: e.activation(
                tb.t[:], zb.t[:], AF.Identity,
                bias=rms[par].t[:, 1:2], scale=rms[par].t[:, 0:1]),
                after=[rms[par].ready, zb.ready] + tb.war())
            rms[par].use(t_n)
            zb.use(t_n)
            tb.set(t_n)
            t_o1 = p.op("dve", lambda e, xb=xb, tb=tb, ob=ob: e.tensor_tensor(
                ob.t[:], tb.t[:], xb.t[:], ALU.add),
                after=[t_n, xb.ready] + ob.war())
            tb.use(t_o1)
            xb.use(t_o1)
            ob.set(t_o1)
            self.store(self.out1_d[gc], ob.t[:], [t_o1], ob)
            self.quant_step_b(qc, [(0, 1)] if par == 0 else [(4, 5)])
        self.quant_flush(qc, [(0, 1)])

    def store(self, dst, src, after, buf=None):
        self._stq.append((dst, src, list(after), buf))

    def flush_stores(self):
        for dst, src, after, buf in self._stq:
            tok = self.p.dma("pool", dst, src, after=after)
            if buf is not None:
                buf.use(tok)
        self._stq = []

    def psum_init(self):
        self._ps = {}

    def psum_war(self, key):
        b = self._ps.get(key)
        return b.war() if b else []

    def psum_set(self, key, tok):
        b = self._ps.setdefault(key, Buf(None))
        b.set(tok)

    def psum_use(self, key, tok):
        b = self._ps.setdefault(key, Buf(None))
        b.use(tok)

    def wcast_alloc(self, ar):
        return dict(stage=[ar.buf([128, E], F32) for _ in range(2)],
                    wb=ar.buf([128, E], BF16), loading=[], k=0)

    def wcast_step(self, wc, w_dram, dst_d, row_scale=None):
        p = self.p
        for kk, st in wc["loading"]:
            wb = wc["wb"]
            if row_scale is None:
                t_c = p.op("act", lambda e, st=st, wb=wb: e.copy(wb.t[:], st.t[:]),
                           after=[st.ready] + wb.war())
            else:
                t_c = p.op("act", lambda e, st=st, wb=wb, kk=kk: e.activation(
                    wb.t[:], st.t[:], AF.Identity, scale=row_scale(kk)),
                    after=[st.ready] + wb.war() + self.TC)
            st.use(t_c)
            wb.set(t_c)
            t_st = p.dma("pool", dst_d[kk], wb.t[:], after=[t_c])
            wb.use(t_st)
        wc["loading"] = []
        if wc["k"] < 16:
            kk = wc["k"]
            wc["k"] += 1
            st = wc["stage"][kk % len(wc["stage"])]
            t_ld = p.dma("sp", st.t[:], w_dram[kk * 128:(kk + 1) * 128, :], after=st.war())
            st.set(t_ld)
            wc["loading"].append((kk, st))

    def load_weight_pre(self, src_d, tiles, nk=16):
        p = self.p
        return [p.dma(("sp", "act")[kk % 2], tiles[kk][:], src_d[kk]) for kk in range(nk)]

    def load_weight_bf16(self, w_dram, tiles, stage, row_scale=None, nk=16):
        p = self.p
        toks = []
        for kk in range(nk):
            st = stage[kk % len(stage)]
            t_ld = p.dma(("sp", "act")[kk % 2], st.t[:],
                         w_dram[kk * 128:(kk + 1) * 128, :], after=st.war())
            st.set(t_ld)
            dst = tiles[kk]
            if row_scale is None:
                t_c = p.op("act", lambda e, st=st, dst=dst: e.copy(dst, st.t[:]), after=[t_ld])
            else:
                t_c = p.op("act", lambda e, st=st, dst=dst, kk=kk: e.activation(
                    dst, st.t[:], AF.Identity, scale=row_scale(kk)), after=[t_ld])
            st.use(t_c)
            toks.append(t_c)
        return toks

    def rotary(self, buf, ci, rot_t1, rot_tmp):
        p = self.p
        cosv = self.cos_sb[:, ci * 32:(ci + 1) * 32].unsqueeze(1).broadcast_to([C, H, 32])
        sinv = self.sin_sb[:, ci * 32:(ci + 1) * 32].unsqueeze(1).broadcast_to([C, H, 32])
        v3 = buf.t[:].rearrange("p (h d) -> p h d", h=H)
        x1v, x2v = v3[:, :, 0:32], v3[:, :, 64:96]
        t1v = rot_t1.t[:].rearrange("p (h d) -> p h d", h=H, d=32)
        tmpv = rot_tmp.t[:].rearrange("p (h d) -> p h d", h=H, d=32)
        t_ev = buf.ready
        t_c1 = p.op("dve", lambda e: e.tensor_copy(t1v, x1v), after=[t_ev] + rot_t1.war())
        buf.use(t_c1)
        rot_t1.set(t_c1)
        t_m1 = p.op("dve", lambda e, sv=sinv: e.tensor_tensor(tmpv, x2v, sv, ALU.mult),
                    after=[t_ev] + rot_tmp.war() + self.TC)
        buf.use(t_m1)
        rot_tmp.set(t_m1)
        t_m2 = p.op("dve", lambda e, cv=cosv: e.tensor_tensor(x1v, x1v, cv, ALU.mult), after=[t_c1, t_ev])
        t_m3 = p.op("dve", lambda e: e.tensor_tensor(x1v, x1v, tmpv, ALU.add), after=[t_m2, t_m1])
        rot_tmp.use(t_m3)
        t_m4 = p.op("dve", lambda e, cv=cosv: e.tensor_tensor(x2v, x2v, cv, ALU.mult), after=[t_m3])
        t_m5 = p.op("dve", lambda e, sv=sinv: e.tensor_tensor(tmpv, t1v, sv, ALU.mult),
                    after=[t_m4, t_c1] + rot_tmp.war())
        rot_t1.use(t_m5)
        rot_tmp.set(t_m5)
        t_rot = p.op("dve", lambda e: e.tensor_tensor(x2v, x2v, tmpv, ALU.subtract), after=[t_m5])
        rot_tmp.use(t_rot)
        buf.set(t_rot)
        return t_rot

    def phase1(self):
        p, ar = self.p, self.ar
        ar.reset()
        self.psum_init()
        wk_bf = [ar.ap([128, E], BF16) for _ in range(16)]
        wv_bf = [ar.ap([128, E], BF16) for _ in range(16)]
        stage = [ar.buf([128, E], F32)]
        kd_sb = ar.buf([C, H * D], BF16)
        x_t = [ar.buf([128, E], F32) for _ in range(2)]
        ln1 = ar.buf([128, E], BF16)
        ln1f = [ar.buf([128, E], BF16) for _ in range(3)]
        ktm = ar.buf([128, E], F32)
        kbtm = ar.buf([128, E], BF16)
        kfm = ar.buf([128, E], BF16)
        kdec = ar.buf([128, E], BF16)
        vbf = ar.buf([128, E], BF16)
        rot_t1 = ar.buf([128, 512], F32)
        rot_tmp = ar.buf([128, 512], F32)
        scrs = [dict(bn=ar.ap([128, 24], F32), mv=ar.ap([128, 2], F32),
                     s4=ar.ap([128, 8], F32)) for _ in range(2)]
        rms = [ar.buf([128, 2], F32) for _ in range(2)]

        chunks = self.chunks
        n = len(chunks)

        def load_x(gi):
            s, ci = chunks[gi]
            xb = x_t[gi % 2]
            t_x = p.dma("sp", xb.t[:], self.x_d[s, ci * C:(ci + 1) * C, :], after=xb.war())
            xb.set(t_x)

        def prep_ln(gi):
            par = gi % 2
            xb = x_t[par]
            t_mb, _ = self.ln_stats(xb, LN_EPS, scrs[par], rms[par].t[:, 0:1],
                                    rms[par].t[:, 1:2], war=rms[par].war())
            rms[par].set(t_mb)
            t_ln = p.op("act", lambda e, xb=xb, par=par: e.activation(
                ln1.t[:], xb.t[:], AF.Identity,
                bias=rms[par].t[:, 1:2], scale=rms[par].t[:, 0:1]),
                after=[t_mb, xb.ready] + ln1.war())
            rms[par].use(t_ln)
            xb.use(t_ln)
            ln1.set(t_ln)
            return t_ln

        def trans_ln(gi, t_ln):
            s, ci = chunks[gi]
            gc = s * NCPS + ci
            lf = ln1f[gi % 3]
            tts = self.transpose16(lambda j: ln1.t[:, j * 128:(j + 1) * 128], lf,
                                   [t_ln] + lf.war(), banks=(2, 3))
            ln1.use(tts[-1])
            self.store(self.ln1f_d[gc], lf.t[:], list(tts), lf)

        load_x(0)
        load_x(1)
        t_kd_c = p.dma("sp", kd_sb.t, self.cst["c_kd"][:])
        kd_sb.set(t_kd_c)
        t_wk = self.load_weight_bf16(self.wk_d, wk_bf, stage)
        t_wv = self.load_weight_bf16(self.wv_d, wv_bf, stage)
        t_ln0 = prep_ln(0)
        trans_ln(0, t_ln0)
        t_ln1 = prep_ln(1)
        trans_ln(1, t_ln1)

        for gi, (s, ci) in enumerate(chunks):
            gc = s * NCPS + ci
            par = gi % 2
            self.flush_stores()
            if gi + 2 < n:
                load_x(gi + 2)
            t_ln_next = prep_ln(gi + 2) if gi + 2 < n else None
            lf = ln1f[gi % 3]
            def evac_k(h, t_mm, ps_half):
                return p.op("act", lambda e, h=h, ps_half=ps_half: e.copy(
                    ktm.t[:, h * 1024:(h + 1) * 1024], ps_half),
                    after=[t_mm] + (ktm.war() if h == 0 else []))

            evs, t_mm = self.mm_evac(
                0, lambda kk, lf=lf: lf.t[:, kk * 128:(kk + 1) * 128],
                lambda kk, nn: wk_bf[kk][:, nn * 512:(nn + 1) * 512], 16, evac_k,
                [lf.ready], gate=lambda nn, kk: [t_wk[kk]])
            lf.use(t_mm)
            ktm.set(evs[-1])

            t_rot = self.rotary(ktm, ci, rot_t1, rot_tmp)

            def evac_v(h, t_mm2, ps_half):
                return p.op("act", lambda e, h=h, ps_half=ps_half: e.copy(
                    vbf.t[:, h * 1024:(h + 1) * 1024], ps_half),
                    after=[t_mm2] + (vbf.war() if h == 0 else []))

            evs_v, t_mmv = self.mm_evac(
                4, lambda kk, lf=lf: lf.t[:, kk * 128:(kk + 1) * 128],
                lambda kk, nn: wv_bf[kk][:, nn * 512:(nn + 1) * 512], 16, evac_v,
                [lf.ready], gate=lambda nn, kk: [t_wv[kk]])
            lf.use(t_mmv)
            vbf.set(evs_v[-1])
            self.store(self.v_d[gc], vbf.t[:], [evs_v[-1]], vbf)

            t_kb = p.op("act", lambda e: e.copy(kbtm.t[:], ktm.t[:]), after=[t_rot] + kbtm.war())
            ktm.use(t_kb)
            kbtm.set(t_kb)
            t_kd = p.op("dve", lambda e: e.tensor_tensor(kdec.t[:], ktm.t[:], kd_sb.t, ALU.mult),
                        after=[t_rot, kd_sb.ready] + kdec.war())
            ktm.use(t_kd)
            kdec.set(t_kd)
            self.store(self.kdec_d[gc], kdec.t[:], [t_kd], kdec)

            tts2 = self.transpose16(lambda j: kbtm.t[:, j * 128:(j + 1) * 128], kfm,
                                    [t_kb] + kfm.war(), banks=(0, 1))
            kbtm.use(tts2[-1])
            self.store(self.kfm_d[gc], kfm.t[:], tts2, kfm)

            if gi + 2 < n:
                trans_ln(gi + 2, t_ln_next)


    def quant_alloc(self, ar, nstage=1):
        return dict(
            stage=[ar.buf([128, E], F32) for _ in range(2)],
            wb=[ar.buf([128, E], BF16) for _ in range(2)],
            tq=[ar.buf([128, E], BF16) for _ in range(nstage)],
            tstage=ar.buf([128, E], BF16),
            gs=ar.ap([128, 32], F32),
            sc=ar.ap([128, 32], BF16),
            inv=ar.ap([128, 32], F32),
            rot=0,
            trot=0,
            loading=[],
            pend=[],
            mathed=[],
        )

    def quant_step_a(self, qc, jobs, n=1):
        p = self.p
        for pend in qc["pend"]:
            self.quant_math_dve(qc, pend)
            qc["mathed"].append(pend)
        qc["pend"] = []
        for pend in qc["loading"]:
            st, wb = pend["st"], pend["wb"]
            t_wb = p.op("act", lambda e, st=st, wb=wb: e.copy(wb.t[:], st.t[:]),
                        after=[st.ready] + wb.war())
            st.use(t_wb)
            wb.set(t_wb)
            qc["pend"].append(pend)
        qc["loading"] = []
        for _ in range(n):
            if jobs:
                qc["loading"].append(self.quant_load(qc, jobs.pop(0)))

    def quant_step_b(self, qc, bankpairs):
        for i, pend in enumerate(qc["mathed"]):
            self.quant_scatter(qc, pend, bankpairs[i % len(bankpairs)])
        qc["mathed"] = []

    def quant_flush(self, qc, bankpairs):
        while qc["pend"] or qc["loading"]:
            self.quant_step_a(qc, [], n=0)
            self.quant_step_b(qc, bankpairs)

    def quant_load(self, qc, job):
        p = self.p
        src_ap, dst_ap = job
        idx = qc["rot"] % 2
        qc["rot"] += 1
        st = qc["stage"][idx % len(qc["stage"])]
        wb = qc["wb"][idx]
        t_ld = p.dma("sp", st.t[:], src_ap, after=st.war())
        st.set(t_ld)
        return dict(dst=dst_ap, st=st, wb=wb)

    def quant_math_dve(self, qc, pend):
        p = self.p
        ng = E // 64
        wb = pend["wb"]
        tq = qc["tq"][qc["trot"] % len(qc["tq"])]
        qc["trot"] += 1
        w3 = wb.t[:].rearrange("p (g n) -> p g n", n=64)
        t_gs = p.op("dve", lambda e, w3=w3: e.tensor_reduce(
            qc["gs"][:, 0:ng], w3, AX.X, ALU.add, apply_absolute_value=True),
            after=[wb.ready])
        wb.use(t_gs)
        t_sc = p.op("dve", lambda e: e.tensor_scalar(
            qc["sc"][:, 0:ng], qc["gs"][:, 0:ng], 1.0 / 64, 1e-8, ALU.mult, ALU.max),
            after=[t_gs])
        t_iv = p.op("dve", lambda e: e.reciprocal(qc["inv"][:, 0:ng], qc["sc"][:, 0:ng]),
                    after=[t_sc])
        inv3 = qc["inv"][:, 0:ng].unsqueeze(2).broadcast_to([128, ng, 64])
        sc3 = qc["sc"][:, 0:ng].unsqueeze(2).broadcast_to([128, ng, 64])
        tq3 = tq.t[:].rearrange("p (g n) -> p g n", n=64)
        t_t = p.op("dve", lambda e, w3=w3, inv3=inv3, tq3=tq3: e.tensor_tensor(
            tq3, w3, inv3, ALU.mult), after=[t_iv, wb.ready] + tq.war())
        wb.use(t_t)
        t_c = p.op("dve", lambda e, tq=tq: e.tensor_scalar(
            tq.t[:], tq.t[:], 1.0, -1.0, ALU.min, ALU.max), after=[t_t])
        t_r = p.op("dve", lambda e, tq=tq: e.tensor_scalar(
            tq.t[:], tq.t[:], MAGIC, -MAGIC, ALU.add, ALU.add), after=[t_c])
        t_qs = p.op("dve", lambda e, tq3=tq3, sc3=sc3: e.tensor_tensor(
            tq3, tq3, sc3, ALU.mult), after=[t_r])
        tq.set(t_qs)
        pend["tq"] = tq

    def quant_scatter(self, qc, pend, banks):
        p = self.p
        tq, ts = pend["tq"], qc["tstage"]
        toks = []
        t_pe = None
        for grp in range(2):
            bno = banks[grp % len(banks)]
            bank = self.pbank(bno).bitcast(BF16)
            for jj in range(8):
                j = grp * 8 + jj
                t_pe = p.op("pe", lambda e, j=j, jj=jj, bank=bank, tq=tq: e.transpose(
                    bank[:, jj * 128:(jj + 1) * 128], tq.t[:, j * 128:(j + 1) * 128],
                    self.ident_bf[:]),
                    after=([tq.ready] + self.psum_war("b%d" % bno) + self.TC) if jj == 0 else [],
                    inc=(jj == 7))
            self.psum_set("b%d" % bno, t_pe)
            t_ev = p.op("act", lambda e, grp=grp, bank=bank: e.copy(
                ts.t[:, grp * 1024:(grp + 1) * 1024], bank[:]),
                after=[t_pe] + (ts.war() if grp == 0 else []))
            self.psum_use("b%d" % bno, t_ev)
            toks.append(t_ev)
        tq.use(t_pe)
        ts.set(toks[-1])
        t_st = p.dma("pool", pend["dst"], ts.t[:], after=toks)
        ts.use(t_st)

    def phase3(self, hf):
        p, ar = self.p, self.ar
        ar.reset()
        self.psum_init()
        HH = HID // 2
        w1_region = ar.ap([128, 16 * 8192], U8)
        w1 = [w1_region[:, i * 8192:(i + 1) * 8192].bitcast(BF16) for i in range(16)]
        h_bf = [ar.buf([128, HID // 2], BF16) for _ in range(1 if hf == 0 else 2)]
        ln2f = [ar.buf([128, E], BF16) for _ in range(2)]
        hscr = dict(bn=ar.ap([128, 48], F32), mv=ar.ap([128, 2], F32), s4=ar.ap([128, 8], F32))
        ms_scr = ar.ap([128, 2], F32)
        if hf == 0:
            o1_t = [ar.buf([128, E], F32) for _ in range(2)]
            ln2 = [ar.buf([128, E], BF16)]
            scrs = [dict(bn=ar.ap([128, 24], F32), mv=ar.ap([128, 2], F32),
                         s4=ar.ap([128, 8], F32)) for _ in range(2)]
            rms = [ar.buf([128, 2], F32) for _ in range(2)]
            qc = self.quant_alloc(ar, nstage=1)
            qjobs = self.qjobs["P3h0"]
            qn = 1
            qbank = [(2, 3)]
        else:
            qc = self.quant_alloc(ar, nstage=2)
            qjobs = self.qjobs["P3h1"]
            qn = 2
            qbank = [(2, 3), (6, 7)]

        chunks = self.chunks
        n = len(chunks)

        def load_o(gi):
            s, ci = chunks[gi]
            gc = s * NCPS + ci
            if hf == 0:
                ob = o1_t[gi % 2]
                t_o = p.dma("sp", ob.t[:], self.out1_d[gc], after=ob.war())
                ob.set(t_o)
            else:
                lf = ln2f[gi % 2]
                t_l2 = p.dma("sp", lf.t[:], self.ln2_d[gc], after=lf.war())
                lf.set(t_l2)

        def prep_ln(gi):
            par = gi % 2
            ob = o1_t[par]
            t_mb, _ = self.ln_stats(ob, LN_EPS, scrs[par], rms[par].t[:, 0:1],
                                    rms[par].t[:, 1:2], war=rms[par].war())
            rms[par].set(t_mb)
            lb = ln2[0]
            t_ln = p.op("act", lambda e, ob=ob, lb=lb, par=par: e.activation(
                lb.t[:], ob.t[:], AF.Identity,
                bias=rms[par].t[:, 1:2], scale=rms[par].t[:, 0:1]),
                after=[t_mb, ob.ready] + lb.war())
            rms[par].use(t_ln)
            ob.use(t_ln)
            lb.set(t_ln)
            return t_ln

        def trans_ln(gi, t_ln):
            s, ci = chunks[gi]
            gc = s * NCPS + ci
            par = gi % 2
            lf = ln2f[par]
            tts = self.transpose16(lambda j, lb=ln2[0]: lb.t[:, j * 128:(j + 1) * 128],
                                   lf, [t_ln] + lf.war(), banks=(0, 1))
            ln2[0].use(tts[-1])
            self.store(self.ln2_d[gc], lf.t[:], list(tts), lf)

        load_o(0)
        load_o(1)
        t_w1 = []
        for kk in range(16):
            wsrc = self.w1q_d[hf][:, :, kk * 128:(kk + 1) * 128]
            ta = p.dma("sp", w1[kk][:, 0:2048].rearrange("p (r c) -> p r c", c=128),
                       wsrc[0:16].rearrange("r p c -> p r c"))
            tb = p.dma("act", w1[kk][:, 2048:4096].rearrange("p (r c) -> p r c", c=128),
                       wsrc[16:32].rearrange("r p c -> p r c"))
            t_w1.append((ta, tb))
        if hf == 0:
            t_ln0 = prep_ln(0)
            trans_ln(0, t_ln0)

        for gi, (s, ci) in enumerate(chunks):
            gc = s * NCPS + ci
            par = gi % 2
            self.flush_stores()
            if hf == 0 and gi + 2 < n:
                load_o(gi + 2)
            if qc is not None:
                self.quant_step_a(qc, qjobs, n=qn)
            t_ln_next = None
            if hf == 0 and gi + 1 < n:
                t_ln_next = prep_ln(gi + 1)
            lf = ln2f[par]
            hb = h_bf[par % len(h_bf)]

            t_sq_toks = []
            t_bn_toks = []
            for slab in range(2):
                def evac(h, t_mm, ps_half, slab=slab, hb=hb):
                    return p.op("act", lambda e, h=h, ps_half=ps_half, slab=slab, hb=hb:
                                e.activation(
                                    hb.t[:, slab * 2048 + h * 1024:slab * 2048 + (h + 1) * 1024],
                                    ps_half, AF.Relu),
                                after=[t_mm] + (hb.war() if (slab == 0 and h == 0) else []))

                evs, t_mm = self.mm_evac(
                    0 if slab == 0 else 4,
                    lambda kk, lf=lf: lf.t[:, kk * 128:(kk + 1) * 128],
                    lambda kk, nn, slab=slab: w1[kk][:, slab * 2048 + nn * 512:
                                                     slab * 2048 + (nn + 1) * 512],
                    16, evac, [lf.ready], gate=lambda nn, kk: list(t_w1[kk]))
                lf.use(t_mm)
                hslab = hb.t[:, slab * 2048:(slab + 1) * 2048]
                t_sq = p.op("dve", lambda e, hslab=hslab: e.tensor_tensor(
                    hslab, hslab, hslab, ALU.mult), after=evs)
                t_sq_toks.append(t_sq)
                for g in range(4):
                    t_bn = p.op("dve", lambda e, slab=slab, g=g, hb=hb: e.bn_stats(
                        hscr["bn"][:, 6 * (slab * 4 + g):6 * (slab * 4 + g + 1)],
                        hb.t[:, slab * 2048 + 512 * g:slab * 2048 + 512 * (g + 1)]),
                        after=[t_sq])
                    t_bn_toks.append(t_bn)
            hb.set(t_sq_toks[-1])
            for t in t_bn_toks:
                hb.use(t)
            self.store(self.h_d[gc][:, hf * HH:(hf + 1) * HH], hb.t[:], t_sq_toks, hb)
            if hf == 1 and gi + 2 < n:
                load_o(gi + 2)
            if hf == 0 and gi + 1 < n:
                trans_ln(gi + 1, t_ln_next)
            if qc is not None:
                self.quant_step_b(qc, qbank)
            t_mv = p.op("dve", lambda e: e.bn_aggr(hscr["mv"][:], hscr["bn"][:, 0:48]),
                        after=[t_bn_toks[-1]])
            t_m2 = p.op("dve", lambda e: e.tensor_tensor(ms_scr[:, 0:1], hscr["mv"][:, 0:1],
                                                         hscr["mv"][:, 0:1], ALU.mult), after=[t_mv])
            if hf == 0:
                t_ms = p.op("dve", lambda e, gc=gc: e.tensor_tensor(
                    self.msum_sb[:, gc:gc + 1], ms_scr[:, 0:1], hscr["mv"][:, 1:2], ALU.add),
                    after=[t_m2])
                self.t_invr2[gc] = t_ms
            else:
                t_msb = p.op("dve", lambda e: e.tensor_tensor(
                    ms_scr[:, 1:2], ms_scr[:, 0:1], hscr["mv"][:, 1:2], ALU.add), after=[t_m2])
                t_tot = p.op("dve", lambda e, gc=gc: e.tensor_tensor(
                    ms_scr[:, 0:1], ms_scr[:, 1:2], self.msum_sb[:, gc:gc + 1], ALU.add),
                    after=[t_msb, self.t_invr2.get(gc)])
                t_a2 = p.op("dve", lambda e: e.tensor_scalar(
                    ms_scr[:, 1:2], ms_scr[:, 0:1], 0.5 / HH, FFN_EPS, ALU.mult, ALU.add),
                    after=[t_tot])
                t_ir2 = self.rstd(ms_scr[:, 1:2], t_a2, self.invr2_sb[:, gc:gc + 1], hscr["s4"][:, 0:4])
                self.t_invr2[gc] = t_ir2
        if qc is not None:
            self.quant_flush(qc, qbank)

    def phase4(self, kf):
        p, ar = self.p, self.ar
        ar.reset()
        self.psum_init()
        HH = HID // 2
        w2_region = ar.ap([128, 32 * 4096], U8)
        w2 = [w2_region[:, i * 4096:(i + 1) * 4096].bitcast(BF16) for i in range(32)]
        ypb = [ar.buf([128, E], F32) for _ in range(2)]
        o1b = [ar.buf([128, E], F32) for _ in range(2)]
        h_u8 = [ar.ap([128, 2 * HH], U8) for _ in range(2)]
        h_t = [Buf(h_u8[i]) for i in range(2)]
        h_bf = [h_u8[i].bitcast(BF16) for i in range(2)]
        h_f = [h_u8[i].bitcast(F32) for i in range(2)]
        hfm = [ar.buf([128, HH], BF16) for _ in range(3)]
        scrs = [dict(bn=ar.ap([128, 24], F32), mv=ar.ap([128, 2], F32),
                     s4=ar.ap([128, 8], F32)) for _ in range(2)]
        rms = [ar.buf([128, 2], F32) for _ in range(2)]

        chunks = self.chunks
        n = len(chunks)

        def load_h(gi):
            s, ci = chunks[gi]
            gc = s * NCPS + ci
            par = gi % 2
            t_h = p.dma("sp", h_bf[par], self.h_d[gc][:, kf * HH:(kf + 1) * HH],
                        after=h_t[par].war())
            h_t[par].set(t_h)

        def load_yo(gi):
            s, ci = chunks[gi]
            gc = s * NCPS + ci
            par = gi % 2
            t_yp = p.dma("sp", ypb[par].t[:], self.yp_d[gc], after=ypb[par].war())
            ypb[par].set(t_yp)
            t_o1 = p.dma("sp", o1b[par].t[:], self.out1_d[gc], after=o1b[par].war())
            o1b[par].set(t_o1)

        def load(gi):
            load_h(gi)
            if kf == 1:
                load_yo(gi)

        def htrans(gi):
            par = gi % 2
            fm = hfm[gi % 3]
            tts = self.transpose16(lambda j, par=par: h_bf[par][:, j * 128:(j + 1) * 128],
                                   fm, [h_t[par].ready] + fm.war(),
                                   nblk=32, banks=(4, 5, 6, 7))
            h_t[par].use(tts[-1])

        load(0)
        load(1)
        htrans_done = 0
        t_w2 = []
        for kk in range(32):
            h2, j = kk // 16, kk % 16
            wsrc = self.w2q_d[kf][h2 * 16:(h2 + 1) * 16, :, j * 128:(j + 1) * 128]
            t_w2.append(p.dma(("sp", "act")[kk % 2],
                              w2[kk][:].rearrange("p (r c) -> p r c", c=128),
                              wsrc.rearrange("r p c -> p r c")))
        htrans(0)
        htrans(1)
        for gi, (s, ci) in enumerate(chunks):
            gc = s * NCPS + ci
            par = gi % 2
            self.flush_stores()
            if gi + 2 < n:
                load_h(gi + 2)
            fm = hfm[gi % 3]

            if kf == 0:
                def evac(h, t_mm, ps_half, par=par, gc=gc):
                    return p.op("act", lambda e, h=h, ps_half=ps_half: e.activation(
                        ypb[par].t[:, h * 1024:(h + 1) * 1024], ps_half, AF.Identity,
                        scale=self.invr2_sb[:, gc:gc + 1]),
                        after=[t_mm, self.t_invr2.get(gc)] + (ypb[par].war() if h == 0 else []))

                evs, t_mm = self.mm_evac(
                    0, lambda kk, fm=fm: fm.t[:, kk * 128:(kk + 1) * 128],
                    lambda kk, nn: w2[kk][:, nn * 512:(nn + 1) * 512], 32, evac,
                    [fm.ready], gate=lambda nn, kk: [t_w2[kk]])
                fm.use(t_mm)
                ypb[par].set(evs[-1])
                if gi + 2 < n:
                    htrans(gi + 2)
                self.store(self.yp_d[gc], ypb[par].t[:], [evs[-1]], ypb[par])
            else:
                def evac(h, t_mm, ps_half, par=par, gc=gc):
                    return p.op("dve", lambda e, h=h, ps_half=ps_half, par=par, gc=gc:
                                e.scalar_tensor_tensor(
                                    ypb[par].t[:, h * 1024:(h + 1) * 1024], ps_half,
                                    self.invr2_sb[:, gc:gc + 1],
                                    ypb[par].t[:, h * 1024:(h + 1) * 1024],
                                    ALU.mult, ALU.add),
                                after=[t_mm, self.t_invr2.get(gc), ypb[par].ready])

                evs, t_mm = self.mm_evac(
                    0, lambda kk, fm=fm: fm.t[:, kk * 128:(kk + 1) * 128],
                    lambda kk, nn: w2[kk][:, nn * 512:(nn + 1) * 512], 32, evac,
                    [fm.ready], gate=lambda nn, kk: [t_w2[kk]])
                fm.use(t_mm)
                if gi + 2 < n:
                    htrans(gi + 2)
                ypb[par].set(evs[-1])
                t_mb, _ = self.ln_stats(ypb[par], LN_EPS, scrs[par], rms[par].t[:, 0:1],
                                        rms[par].t[:, 1:2], war=rms[par].war())
                rms[par].set(t_mb)
                t_n = p.op("act", lambda e, par=par: e.activation(
                    ypb[par].t[:], ypb[par].t[:], AF.Identity,
                    bias=rms[par].t[:, 1:2], scale=rms[par].t[:, 0:1]),
                    after=[t_mb, ypb[par].ready])
                rms[par].use(t_n)
                ypb[par].set(t_n)
                t_out = p.op("dve", lambda e, par=par: e.tensor_tensor(
                    ypb[par].t[:], ypb[par].t[:], o1b[par].t[:], ALU.add),
                    after=[t_n, o1b[par].ready])
                o1b[par].use(t_out)
                ypb[par].set(t_out)
                self.store(self.out_d[s, ci * C:(ci + 1) * C, :], ypb[par].t[:],
                           [t_out], ypb[par])
                if gi + 2 < n:
                    self.flush_stores()
                    load_yo(gi + 2)

    def finish(self):
        p = self.p
        if self.cfg["debug_scratch"]:
            dbg = self.nc.dram_tensor("s_dbg", [128, 2 * NCH], F32, kind="ExternalOutput").ap()
            p.dma("sp", dbg[:, 0:NCH], self.invr_sb[:],
                  after=[(s, p.cnt[s]) for s in ("dve", "act", "pe")])
            p.dma("sp", dbg[:, NCH:2 * NCH], self.invr2_sb[:],
                  after=[(s, p.cnt[s]) for s in ("dve", "act", "pe")])
        p.fence("sp", [(s, p.cnt[s]) for s in SEM_NAMES if s.startswith("d_")])
        es = contextlib.ExitStack()
        sems = {s: es.enter_context(self.nc.semaphore(s)) for s in SEM_NAMES}
        with es:
            p.emit(self.nc, sems)
        return self.nc


def build_nc(cfg):
    k = K(cfg)
    phases = []
    if "P1" in cfg["phases"]:
        phases.append(k.phase1)
    if "P2" in cfg["phases"]:
        phases.append(k.phase2)
    if "P2b" in cfg["phases"]:
        phases.append(k.phase2b)
    if "P3" in cfg["phases"]:
        phases.append(lambda: k.phase3(0))
        phases.append(lambda: k.phase3(1))
    if "P4" in cfg["phases"]:
        phases.append(lambda: k.phase4(0))
        phases.append(lambda: k.phase4(1))
    for ph in phases:
        ph()
        k.flush_stores()
        k.p.barrier()
    return k.finish(), k.consts


def _prep_inputs(inputs):
    consts, _ = _host_constants()
    rwt = np.ascontiguousarray(inputs["L0_attn_rms_w"].reshape(16, 128).T)
    base = dict(
        wq=np.ascontiguousarray(inputs["L0_attn_Wq"]),
        wk=np.ascontiguousarray(inputs["L0_attn_Wk"]),
        wv=np.ascontiguousarray(inputs["L0_attn_Wv"]),
        wo=np.ascontiguousarray(inputs["L0_attn_Wo"]),
        rwt=rwt,
        wfc=np.ascontiguousarray(inputs["L0_ffn_W_fc"]),
        wpr=np.ascontiguousarray(inputs["L0_ffn_W_proj"]),
    )
    base.update(consts)
    return base


def kernel(**inputs):
    x = np.ascontiguousarray(inputs["L0_attn_x"])
    cores = CFG["cores"] if CFG["cores"] is not None else list(range(8))
    base = _prep_inputs(inputs)
    nc, _ = build_nc(CFG)
    in_maps = []
    for i in range(len(cores)):
        m = dict(base)
        m["x"] = np.ascontiguousarray(x[i * NSEQ:(i + 1) * NSEQ])
        in_maps.append(m)
    res = run_bass_kernel_spmd(nc, in_maps, core_ids=cores)
    return np.concatenate([r["out"] for r in res.results], axis=0)
